# version 09
# baseline (speedup 1.0000x reference)
import contextlib
from collections import deque

import numpy as np
import concourse.bass as bass
import concourse.mybir as mybir
from concourse.bass_utils import run_bass_kernel_spmd

F32 = mybir.dt.float32
BF16 = mybir.dt.bfloat16
FP8 = mybir.dt.float8e4
U8 = mybir.dt.uint8
AF = mybir.ActivationFunctionType
ALU = mybir.AluOpType
AX = mybir.AxisListType

T, D, NH, HDim, NE, HID, GS = 2048, 1024, 16, 64, 16, 512, 256
NG = T // GS
ENGINES = ("sync", "vector", "scalar", "tensor", "gpsimd")


class B:
    def __init__(self):
        self.ops = []
        self.eng_ops = {e: [] for e in ENGINES}
        self.last_w = {}
        self.readers = {}
        self.bar_groups = []

    def add(self, engine, emit, reads=(), writes=(), dma=False):
        idx = len(self.ops)
        deps = set()
        for k in reads:
            if k in self.last_w:
                deps.add(self.last_w[k])
        for k in writes:
            if k in self.last_w:
                deps.add(self.last_w[k])
            for r in self.readers.get(k, ()):
                deps.add(r)
        self.ops.append({"eng": engine, "emit": emit, "deps": deps,
                         "dma": dma, "idx": idx, "bar": False})
        self.eng_ops[engine].append(idx)
        for k in writes:
            self.last_w[k] = idx
            self.readers[k] = []
        for k in reads:
            self.readers.setdefault(k, []).append(idx)
        return idx

    def dma(self, engine, dst, src, reads=(), writes=(), accum=False, slow=False):
        kw = {}
        if accum:
            kw["accum_op"] = ALU.add
        if slow:
            kw["allow_slow_non_contiguous"] = True
        emit = lambda e, _d=dst, _s=src, _k=kw: e.dma_start(out=_d, in_=_s, **_k)
        return self.add(engine, emit, reads=reads, writes=writes, dma=True)

    def barrier(self, exclude=()):
        pre = set()
        for e in ENGINES:
            if e in exclude:
                continue
            if self.eng_ops[e]:
                pre.add(self.eng_ops[e][-1])
        group = []
        for e in ENGINES:
            if e in exclude:
                continue
            i = len(self.ops)
            self.ops.append({"eng": e, "emit": None, "deps": set(pre),
                             "dma": False, "idx": i, "bar": True})
            self.eng_ops[e].append(i)
            group.append(i)
        self.bar_groups.append(group)
        self.last_w = {}
        self.readers = {}

    def finalize(self, nc, block):
        ops = self.ops
        n = len(ops)
        for o in ops:
            if o["bar"]:
                continue
            for g in self.bar_groups:
                if g[-1] < o["idx"]:
                    o["deps"].update(g)
        eng_pos = {}
        for e in ENGINES:
            for pos, i in enumerate(self.eng_ops[e]):
                eng_pos[i] = pos
        for o in ops:
            byeng = {}
            sdeps = []
            for d in o["deps"]:
                de = ops[d]["eng"]
                dk = ops[d]["dma"]
                if de == o["eng"] and (not dk or o["dma"]):
                    if not dk and not o["dma"] and not o["bar"] and not ops[d]["bar"]:
                        sdeps.append(d)
                    continue
                key = (de, dk)
                if key not in byeng or eng_pos[d] > eng_pos[byeng[key]]:
                    byeng[key] = d
            o["x"] = byeng
            o["sdeps"] = sdeps
        adj = [[] for _ in range(n)]
        indeg = [0] * n
        for e in ENGINES:
            seq = self.eng_ops[e]
            for a_, b_ in zip(seq, seq[1:]):
                adj[a_].append(b_)
                indeg[b_] += 1
        for o in ops:
            for d in o["x"].values():
                adj[d].append(o["idx"])
                indeg[o["idx"]] += 1
        dq = deque(i for i in range(n) if indeg[i] == 0)
        seen = 0
        while dq:
            u = dq.popleft()
            seen += 1
            for v in adj[u]:
                indeg[v] -= 1
                if indeg[v] == 0:
                    dq.append(v)
        if seen != n:
            raise RuntimeError(f"dependency cycle: {n - seen}/{n} ops stuck")
        marked = set()
        for o in ops:
            marked.update(o["x"].values())
        for e in ENGINES:
            for i in self.eng_ops[e]:
                if ops[i]["bar"] or ops[i]["dma"]:
                    marked.add(i)
        fenced = set()
        for o in ops:
            for d in o["x"].values():
                if ops[d]["dma"]:
                    fenced.add(d)
            for d in o["deps"]:
                if ops[d]["dma"] and ops[d]["eng"] == o["eng"] and o["dma"]:
                    fenced.add(d)
        inc_tgt = {}
        fence_val = {}
        ci = {e: 0 for e in ENGINES}
        cd = {e: 0 for e in ENGINES}
        for e in ENGINES:
            for i in self.eng_ops[e]:
                if i in marked:
                    if ops[i]["dma"]:
                        cd[e] += 16
                        inc_tgt[i] = ("d", cd[e])
                        if i in fenced:
                            ci[e] += 1
                            fence_val[i] = ci[e]
                    else:
                        ci[e] += 1
                        inc_tgt[i] = ("i", ci[e])
        sem_i = {e: nc.semaphore(f"si{e}").__enter__() for e in ENGINES}
        sem_d = {e: nc.semaphore(f"sd{e}").__enter__() for e in ENGINES}
        self_ = self

        junk_v, junk_s, junk_g = self.junk
        GAP = 3

        def run(eng_obj, ename):
            wi = {e: 0 for e in ENGINES}
            wd = {e: 0 for e in ENGINES}
            emit_pos = {}
            pos = 0

            def dummy():
                nonlocal pos
                if ename == "scalar":
                    eng_obj.memzero(junk_s[:, 0:32])
                elif ename == "vector":
                    eng_obj.memset(junk_v[:, 0:32], 0.0)
                elif ename == "gpsimd":
                    eng_obj.memset(junk_g[:, 0:32], 0.0)
                else:
                    eng_obj.nop()
                pos += 1

            for i in self_.eng_ops[ename]:
                o = ops[i]
                for d in o.get("sdeps", ()):
                    need = GAP - (pos - emit_pos[d])
                    for _ in range(max(0, need)):
                        dummy()
                for key in sorted(o["x"]):
                    d = o["x"][key]
                    de = key[0]
                    if d in fence_val and not (o["dma"] and ename == de):
                        kind, val = "i", fence_val[d]
                    else:
                        kind, val = inc_tgt[d]
                    if kind == "i":
                        if wi[de] < val:
                            eng_obj.wait_ge(sem_i[de], val)
                            wi[de] = val
                    else:
                        if wd[de] < val:
                            eng_obj.wait_ge(sem_d[de], val)
                            wd[de] = val
                if o["bar"]:
                    eng_obj.nop().then_inc(sem_i[ename], 1)
                    emit_pos[i] = pos
                    continue
                ins = o["emit"](eng_obj)
                emit_pos[i] = pos
                pos += 1
                if i in marked:
                    ins.then_inc(sem_d[ename] if o["dma"] else sem_i[ename],
                                 16 if o["dma"] else 1)
                if i in fence_val:
                    eng_obj.wait_ge(sem_d[ename], inc_tgt[i][1])
                    wd[ename] = max(wd[ename], inc_tgt[i][1])
                    eng_obj.nop().then_inc(sem_i[ename], 1)

        block.sync(lambda e: run(e, "sync"))
        block.vector(lambda e: run(e, "vector"))
        block.scalar(lambda e: run(e, "scalar"))
        block.tensor(lambda e: run(e, "tensor"))
        block.gpsimd(lambda e: run(e, "gpsimd"))


def build_nc():
    import os
    UPTO = float(os.environ.get("K_UPTO", "99"))
    nc = bass.Bass("TRN2", target_bir_lowering=False)
    b = B()

    def din(name, shape, dtype=F32):
        return nc.dram_tensor(name, list(shape), dtype, kind="ExternalInput").ap()

    x_h = din("x", (T, D))
    wqT = din("wqT", (D, D)); wkT = din("wkT", (D, D)); wvT = din("wvT", (D, D))
    woutT = din("woutT", (D, D)); bout_h = din("bout", (1, D))
    gwT = din("gate_wT", (D, NE))
    w1Th = din("w1T", (NE, D, HID)); w3Th = din("w3T", (NE, D, HID))
    w2Th = din("w2T", (NE, HID, D))
    sw1T = din("sw1T", (D, HID)); sw3T = din("sw3T", (D, HID)); sw2T = din("sw2T", (HID, D))
    wqkv = din("wqkv", (D, 3 * D)); lnw_h = din("ln_w", (1, D))
    wproj = din("wproj", (D, D))
    fWT = din("fWT", (D, D)); fbias_h = din("fbias", (1, D))
    ident_h = din("ident", (128, 128))
    onesr_h = din("ones_r", (1, 128))
    bd_h = din("bd", (D, NH))
    maskT_h = din("maskT", (GS, GS))
    out_h = nc.dram_tensor("out", [T, D], F32, kind="ExternalOutput").ap()

    NBIG = 2 * D * T + 16
    big_l = nc.dram_tensor("big_l", [NBIG], F32).ap()
    big_g = nc.dram_tensor("big_g", [4, NBIG], F32).ap()
    k_loc = big_l[0:D * T].rearrange("(r c) -> r c", c=T)
    vt_loc = big_l[D * T:2 * D * T].rearrange("(r c) -> r c", c=D)
    kn_loc = big_l[2 * D * T:2 * D * T + 16].rearrange("(o h) -> o h", o=1)
    k_locb = big_l[0:D * T].bitcast(BF16).rearrange("(s r c) -> s r c", s=2, c=T)
    k_gv = lambda s: big_g[s, 0:D * T].rearrange("(r c) -> r c", c=T)
    k_gvb = lambda s: big_g[s, 0:D * T].bitcast(BF16).rearrange("(p r c) -> p r c", p=2, c=T)
    vt_gv = lambda s: big_g[s, D * T:2 * D * T].rearrange("(r c) -> r c", c=D)
    kn_gv = lambda s: big_g[s, 2 * D * T:2 * D * T + 16].rearrange("(o h) -> o h", o=1)
    q_sp = nc.dram_tensor("q_sp", [D * T], F32).ap()
    q_spb = q_sp.bitcast(BF16).rearrange("(s r c) -> s r c", s=2, c=T)
    g_d = nc.dram_tensor("g_d", [T, D], F32).ap()
    s_d = nc.dram_tensor("s_d", [T, D], F32).ap()
    s1_d = nc.dram_tensor("s1_d", [T, D], F32).ap()
    x1_d = nc.dram_tensor("x1_d", [T, D], F32).ap()
    wqkvp = nc.dram_tensor("wqkvp", [2, 24, 128, 512], F32).ap()
    wqkvpb = wqkvp.bitcast(BF16)
    am_l = nc.dram_tensor("am_l", [1, 1], F32).ap()
    am_gd = nc.dram_tensor("am_g", [1, 1], F32).ap()

    GRP = [[0, 1, 2, 3], [4, 5, 6, 7]]
    ALL8 = [list(range(8))]

    ctx = contextlib.ExitStack()
    e_ = ctx.enter_context
    U0 = e_(nc.sbuf_tensor("U0", [128, 16384], F32))
    U1 = e_(nc.sbuf_tensor("U1", [128, 16384], F32))
    U2 = e_(nc.sbuf_tensor("U2", [128, 4096], F32))
    U3 = e_(nc.sbuf_tensor("U3", [128, 512], F32))
    U4 = e_(nc.sbuf_tensor("U4", [128, 512], F32))
    CN = e_(nc.sbuf_tensor("CN", [128, 1024], F32))
    BBB = e_(nc.sbuf_tensor("BBB", [128, 2048], F32))
    IDn = e_(nc.sbuf_tensor("IDn", [128, 128], F32))
    MK = e_(nc.sbuf_tensor("MK", [128, 512], F32))
    XQB = e_(nc.sbuf_tensor("XQB", [128, 12288], BF16))
    F8s = e_(nc.sbuf_tensor("F8s", [128, 1024], FP8))
    M8 = e_(nc.sbuf_tensor("M8", [128, 1024], U8))
    junk_v = e_(nc.sbuf_tensor("junk_v", [128, 32], F32))
    junk_s = e_(nc.sbuf_tensor("junk_s", [128, 32], F32))
    junk_g = e_(nc.sbuf_tensor("junk_g", [128, 32], F32))
    PS = [e_(nc.psum_tensor(f"P{i}", [128, 512], F32)) for i in range(8)]
    PK = [f"P{i}" for i in range(8)]

    def u0c(c, a, z):
        return U0[:, c * T + a: c * T + z]

    U0B = U0.ap().bitcast(BF16)

    def xh(c, a, z):
        return U0B[:, c * T + a: c * T + z]

    def xl(c, a, z):
        return U0B[:, 8 * T + c * T + a: 8 * T + c * T + z]

    def colblocks(dst2d, src2d, nc_):
        m = src2d.shape[1]
        return (dst2d.rearrange("p (n m) -> p n m", n=nc_),
                src2d.rearrange("(n p) m -> p n m", p=128))

    def mm(ps, lhsT, rhs, start, stop, reads, writes):
        b.add("tensor", lambda e, _o=ps, _l=lhsT, _r=rhs, _s=start, _p=stop:
              e.matmul(_o, _l, _r, start=_s, stop=_p), reads=reads, writes=writes)

    def vop(emit, reads, writes):
        b.add("vector", emit, reads=reads, writes=writes)

    def sop(emit, reads, writes):
        b.add("scalar", emit, reads=reads, writes=writes)

    b.dma("sync", IDn[:, :], ident_h, writes=["IDn"])
    b.dma("sync", CN[0:1, 0:128], onesr_h, writes=["ones"])
    b.dma("sync", *colblocks(CN[:, 128:256], bd_h, 8), writes=["bd"])
    b.dma("sync", *colblocks(MK[:, 0:512], maskT_h, 2), writes=["MK"])
    b.dma("sync", CN[:, 256:264].rearrange("p (c o) -> p c o", c=8),
          lnw_h.rearrange("o (c p) -> p c o", p=128), writes=["lnw"], slow=True)
    b.dma("sync", CN[0:1, 264:264 + 128], onesr_h, writes=["_pad"])
    vop(lambda e: e.tensor_mul(CN[:, 272:280], CN[:, 256:264], CN[:, 256:264]),
        ["lnw"], ["lnw2"])
    vop(lambda e: e.memset(CN[:, 448:449], 1.0), [], ["onescol"])
    b.barrier()

    def ln_16tiles(load_tile, dst_chunk, eps, tps, amax_slot=None):
        for tt in range(16):
            j = tt % 2
            xt = U2[:, 2048 + j * 1024: 2048 + (j + 1) * 1024]
            xk = f"lnx{j}"
            stat = lambda k, _j=j: U3[0:128, _j * 16 + k:_j * 16 + k + 1]
            sk = f"lns{j}"
            load_tile(tt, xt, xk)
            sop(lambda e, _x=xt, _a=stat(0): e.activation(
                BBB[:, 0:1024], _x, AF.Square, bias=0.0, scale=1.0, accum_out=_a),
                [xk], ["lnsq", sk + "a"])
            vop(lambda e, _x=xt, _s=stat(1): e.reduce_sum(_s, _x, axis=AX.X),
                [xk], [sk + "b"])
            vop(lambda e, _s=stat(1), _m=stat(2): e.tensor_scalar_mul(_m, _s, 1.0 / D),
                [sk + "b"], [sk + "m"])
            vop(lambda e, _m=stat(2), _t=stat(3): e.tensor_mul(_t, _m, _m),
                [sk + "m"], [sk + "t"])
            vop(lambda e, _a=stat(0), _t=stat(3), _v=stat(4): e.scalar_tensor_tensor(
                _v, _a, 1.0 / D, _t, ALU.mult, ALU.subtract),
                [sk + "a", sk + "t"], [sk + "v"])
            vop(lambda e, _v=stat(4), _e=eps: e.tensor_scalar_add(_v, _v, _e),
                [sk + "v"], [sk + "v"])
            vop(lambda e, _v=stat(4), _r=stat(5): e.reciprocal(_r, _v),
                [sk + "v"], [sk + "r0"])
            sop(lambda e, _r=stat(5): e.activation(_r, _r, AF.Sqrt, bias=0.0,
                                                   scale=1.0),
                [sk + "r0"], [sk + "r"])
            import os as _os2
            _stat = _os2.environ.get("K_STAT", "")
            if _os2.environ.get("K_RAWX"):
                pass
            elif _stat == "m":
                vop(lambda e, _x=xt, _m=stat(2): e.tensor_scalar(
                    _x, _x, _m, None, ALU.subtract),
                    [xk, sk + "m"], [xk])
            elif _stat == "r":
                vop(lambda e, _x=xt, _r=stat(5): e.tensor_scalar(
                    _x, _x, _r, None, ALU.mult),
                    [xk, sk + "r"], [xk])
            elif amax_slot is None:
                vop(lambda e, _x=xt, _m=stat(2), _r=stat(5): e.tensor_scalar(
                    _x, _x, _m, _r, ALU.subtract, ALU.mult),
                    [xk, sk + "m", sk + "r"], [xk])
            if amax_slot is not None:
                vop(lambda e, _x=xt, _m=stat(2), _r=stat(5): e.tensor_scalar(
                    _x, _x, _m, _r, ALU.subtract, ALU.mult),
                    [xk, sk + "m", sk + "r"], [xk])
                vop(lambda e, _x=xt, _t=stat(6): e.tensor_reduce(
                    _t, _x, axis=AX.X, op=ALU.max, apply_absolute_value=True),
                    [xk], [sk + "am"])
                if tt == 0:
                    vop(lambda e, _t=stat(6), _d=amax_slot: e.tensor_copy(_d, _t),
                        [sk + "am"], ["runamax"])
                else:
                    vop(lambda e, _t=stat(6), _d=amax_slot: e.tensor_max(_d, _d, _t),
                        [sk + "am", "runamax"], ["runamax"])
            for c in range(8):
                pt, pk = PS[tps[c % 2]], PK[tps[c % 2]]
                mm(pt[:, 0:128], xt[:, c * 128:(c + 1) * 128], IDn[:, :],
                   True, True, [xk, "IDn"], [pk])
                hi, lo, dkey = dst_chunk(c, tt)
                vop(lambda e, _d=hi, _p=pt: e.tensor_copy(_d, _p[:, 0:128]),
                    [pk], [dkey + "h"])
                vop(lambda e, _d=lo, _h=hi, _p=pt: e.tensor_tensor(
                    _d, _p[:, 0:128], _h, ALU.subtract),
                    [pk, dkey + "h"], [dkey + "l"])

    class _Done(Exception):
        pass

    def _ck(n):
        if UPTO < n:
            _tapflush()
            raise _Done()

    _taps = []

    def tap(ap, reads):
        _taps.append((ap, list(reads)))

    def _tapflush():
        off = 0
        of = out_h.rearrange("r c -> (r c)")
        for ap, reads in _taps:
            p, w = ap.shape[0], ap.shape[1]
            dst = of[off:off + p * w].rearrange("(p m) -> p m", p=p)
            b.dma("gpsimd", dst, ap, reads=reads, writes=["__tap__"])
            off += p * w
        b.barrier()

    try:
        _ck(1) if False else None
        def load_x(tt, xt, xk):
            b.dma("sync", xt, x_h[tt * 128:(tt + 1) * 128, :], reads=[xk], writes=[xk])

        _ck(1)
        ln_16tiles(load_x,
                   lambda c, tt: (xh(c, tt * 128, (tt + 1) * 128),
                                  xl(c, tt * 128, (tt + 1) * 128), f"xn.c{c}"),
                   1e-5, (0, 1))

        U2B = U2.ap().bitcast(BF16)
        BBBB = BBB.ap().bitcast(BF16)
        WF32 = U2[:, 0:1024]
        WHB = U2B[:, 2048:3072]
        WLB = U2B[:, 3072:4096]

        def wsplit(wT_d, dc, wk):
            b.dma("sync", *colblocks(WF32, wT_d[:, dc * 128:(dc + 1) * 128], 8),
                  reads=[wk + "f"], writes=[wk + "f"])
            vop(lambda e: e.tensor_copy(WHB[:, :], WF32), [wk + "f", wk], [wk + "h"])
            vop(lambda e: e.tensor_tensor(WLB[:, :], WF32, WHB[:, :], ALU.subtract),
                [wk + "f", wk + "h", wk], [wk + "l"])

        def mm3(pt, pk, wkk, kc, rhs_h, rhs_l, rkeys, first, last):
            wh = WHB[:, kc * 128:(kc + 1) * 128]
            wl = WLB[:, kc * 128:(kc + 1) * 128]
            mm(pt, wh, rhs_h, first, False, [wkk + "h"] + rkeys, [pk])
            mm(pt, wh, rhs_l, False, False, [wkk + "h"] + rkeys, [pk])
            mm(pt, wl, rhs_h, False, last, [wkk + "l"] + rkeys, [pk])

        def proj_l0(wT_d, post, norm_reduce_key=None):
            npsi = [2, 3, 6, 7]
            for dc in range(8):
                wk = "wblk"
                wsplit(wT_d, dc, wk)
                for tk in range(4):
                    pt, pk = PS[4 + (tk % 2)], PK[4 + (tk % 2)]
                    for kc in range(8):
                        mm3(pt[:, :], pk, wk, kc,
                            xh(kc, tk * 512, (tk + 1) * 512),
                            xl(kc, tk * 512, (tk + 1) * 512),
                            [f"xn.c{kc}h", f"xn.c{kc}l"], kc == 0, kc == 7)
                    post(dc, tk, pt, pk, npsi[tk])
                vop(lambda e: e.memset(CN[:, 1023:1024], 0.0),
                    [wk + "h", wk + "l"], [wk])
            if norm_reduce_key is None:
                return
            for tk in range(4):
                vop(lambda e, _tk=tk, _n=npsi[tk], _k=norm_reduce_key: e.tensor_reduce(
                    CN[0:16, 300 + _tk:301 + _tk], PS[_n][0:16, 0:512],
                    axis=AX.X, op=ALU.max), [PK[npsi[tk]]], [f"{norm_reduce_key}{tk}"])
            vop(lambda e: e.tensor_max(CN[0:16, 300:301], CN[0:16, 300:301],
                                       CN[0:16, 301:302]),
                [f"{norm_reduce_key}0", f"{norm_reduce_key}1"], [f"{norm_reduce_key}0"])
            vop(lambda e: e.tensor_max(CN[0:16, 302:303], CN[0:16, 302:303],
                                       CN[0:16, 303:304]),
                [f"{norm_reduce_key}2", f"{norm_reduce_key}3"], [f"{norm_reduce_key}2"])
            vop(lambda e: e.tensor_max(CN[0:16, 304:305], CN[0:16, 300:301],
                                       CN[0:16, 302:303]),
                [f"{norm_reduce_key}0", f"{norm_reduce_key}2"], [f"{norm_reduce_key}M"])

        def sq_norm_mm(src_ap, src_key, tk, dc, npi):
            sq = U4[:, 0:512]
            sop(lambda e, _s=src_ap, _q=sq: e.activation(
                _q, _s, AF.Square, bias=0.0, scale=1.0), [src_key], ["nsq"])
            mm(PS[npi][0:16, 0:512], CN[:, 128 + dc * 16:128 + (dc + 1) * 16], sq,
               dc == 0, dc == 7, ["nsq", "bd"], [PK[npi]])

        U1B = U1.ap().bitcast(BF16)

        QDBG = os.environ.get("K_QDBG", "")

        def q_post(dc, tk, pt, pk, npi):
            dh = U1B[:, dc * T + tk * 512: dc * T + (tk + 1) * 512]
            dl = U1B[:, 8 * T + dc * T + tk * 512: 8 * T + dc * T + (tk + 1) * 512]
            dk = f"q.{dc}.{tk}"
            vop(lambda e, _d=dh, _p=pt: e.tensor_copy(_d, _p[:, :]), [pk], [dk + "h"])
            vop(lambda e, _d=dl, _h=dh, _p=pt: e.tensor_tensor(
                _d, _p[:, :], _h, ALU.subtract), [pk, dk + "h"], [dk + "l"])
            if QDBG != "nonorm":
                sq_norm_mm(dh, dk + "h", tk, dc, npi)
            if QDBG != "nospill":
                b.dma("scalar", q_spb[0, dc * 128:(dc + 1) * 128, tk * 512:(tk + 1) * 512],
                      dh, reads=[dk + "h"], writes=[f"qsp{dc}.{tk}"])
                b.dma("scalar", q_spb[1, dc * 128:(dc + 1) * 128, tk * 512:(tk + 1) * 512],
                      dl, reads=[dk + "l"], writes=[f"qspl{dc}.{tk}"])

        if UPTO < 3:
            import os as _os3
            nch = 8
            if _os3.environ.get("K_STTAP"):
                tap(U3[0:128, 0:32], ["lns0m", "lns1m", "lns0b", "lns1b"])
                nch = 7
            for c in range(nch):
                tap(u0c(c, 0, T), [f"xn.c{c}"])
        _ck(2)
        proj_l0(wqT, q_post, "qn")
        mm(PS[2][0:1, 0:16], CN[0:16, 304:305], IDn[0:16, 0:16], True, True,
           ["qnM", "IDn"], [PK[2]])
        sop(lambda e: e.activation(CN[0:1, 320:336], PS[2][0:1, 0:16], AF.Sqrt,
                                   bias=0.0, scale=1.0), [PK[2]], ["qmr"])

        def k_post(dc, tk, pt, pk, npi):
            j = tk % 2
            sh = BBBB[:, 2048 + j * 512:2048 + j * 512 + 512]
            sl = BBBB[:, 3072 + j * 512:3072 + j * 512 + 512]
            sk = f"pstg{j}"
            vop(lambda e, _s=sh, _p=pt: e.tensor_copy(_s, _p[:, :]), [pk], [sk + "h"])
            vop(lambda e, _s=sl, _h=sh, _p=pt: e.tensor_tensor(
                _s, _p[:, :], _h, ALU.subtract), [pk, sk + "h"], [sk + "l"])
            b.dma("scalar", k_locb[0, dc * 128:(dc + 1) * 128, tk * 512:(tk + 1) * 512],
                  sh, reads=[sk + "h"], writes=[f"kd{dc}.{tk}"])
            b.dma("scalar", k_locb[1, dc * 128:(dc + 1) * 128, tk * 512:(tk + 1) * 512],
                  sl, reads=[sk + "l"], writes=[f"kdl{dc}.{tk}"])
            sq_norm_mm(sh, sk + "h", tk, dc, npi)

        _ck(2.25)
        proj_l0(wkT, k_post, "kn")
        mm(PS[2][0:1, 0:16], CN[0:16, 304:305], IDn[0:16, 0:16], True, True,
           ["knM", "IDn"], [PK[2]])
        sop(lambda e: e.activation(CN[0:1, 336:352], PS[2][0:1, 0:16], AF.Sqrt,
                                   bias=0.0, scale=1.0), [PK[2]], ["kmr"])
        b.dma("scalar", kn_loc, CN[0:1, 336:352], reads=["kmr"], writes=["knd"])

        VTEST = os.environ.get("K_VTEST", "full")

        def v_post(dc, tk, pt, pk, npi):
            stg = BBB[:, 1024 + (tk % 2) * 512: 1024 + (tk % 2 + 1) * 512]
            sk = f"pstg{tk % 2}"
            vop(lambda e, _s=stg, _p=pt: e.tensor_copy(_s, _p[:, :]),
                [pk, sk + "h", sk + "l"], [sk, sk + "h", sk + "l"])
            for ts in range(4):
                if VTEST in ("full", "nodma"):
                    mm(PS[0][:, 0:128], stg[:, ts * 128:(ts + 1) * 128], IDn[:, :],
                       True, True, [sk, "IDn"], [PK[0]])
                vstg = CN[:, 768:896]
                if VTEST in ("full", "nodma"):
                    vop(lambda e, _p=PS[0], _v=vstg: e.tensor_copy(_v, _p[:, 0:128]),
                        [PK[0]], ["vstg"])
                if VTEST == "full":
                    b.dma("gpsimd",
                          vt_loc[tk * 512 + ts * 128: tk * 512 + (ts + 1) * 128,
                                 dc * 128:(dc + 1) * 128],
                          vstg, reads=["vstg"], writes=[f"vtd{dc}.{tk}.{ts}"])
                elif VTEST == "dmaonly":
                    vop(lambda e, _v=vstg: e.memset(_v, 1.0), [], ["vstg"])
                    b.dma("gpsimd",
                          vt_loc[tk * 512 + ts * 128: tk * 512 + (ts + 1) * 128,
                                 dc * 128:(dc + 1) * 128],
                          vstg, reads=["vstg"], writes=[f"vtd{dc}.{tk}.{ts}"])

        _ck(2.5)
        proj_l0(wvT, v_post)

        _ck(2.75)
        kd_keys = [f"kd{dc}.{tk}" for dc in range(8) for tk in range(4)]
        vt_keys = [f"vtd{dc}.{tk}.{ts}" for dc in range(8) for tk in range(4)
                   for ts in range(4)]
        b.add("gpsimd", lambda e: e.collective_compute(
            "AllGather", ALU.bypass, replica_groups=GRP, ins=[big_l], outs=[big_g]),
            reads=kd_keys + vt_keys + ["knd"], writes=["k_g", "vt_g", "kn_g"])
        b.barrier()

        _ck(3)
        b.dma("sync", *colblocks(U2[:, 2048:2176], gwT, 8), writes=["gwf"])
        U2B3 = U2.ap().bitcast(BF16)
        GWH = U2B3[:, 4352:4480]
        GWL = U2B3[:, 4480:4608]
        vop(lambda e: e.tensor_copy(GWH[:, :], U2[:, 2048:2176]), ["gwf"], ["gwh"])
        vop(lambda e: e.tensor_tensor(GWL[:, :], U2[:, 2048:2176], GWH[:, :],
                                      ALU.subtract), ["gwf", "gwh"], ["gwl"])
        for tc in range(16):
            pt, pk = PS[6], PK[6]
            for kc in range(8):
                mm(pt[:, 0:16], xh(kc, tc * 128, (tc + 1) * 128),
                   GWH[:, kc * 16:(kc + 1) * 16], kc == 0, False,
                   [f"xn.c{kc}h", "gwh"], [pk])
                mm(pt[:, 0:16], xl(kc, tc * 128, (tc + 1) * 128),
                   GWH[:, kc * 16:(kc + 1) * 16], False, False,
                   [f"xn.c{kc}l", "gwh"], [pk])
                mm(pt[:, 0:16], xh(kc, tc * 128, (tc + 1) * 128),
                   GWL[:, kc * 16:(kc + 1) * 16], False, kc == 7,
                   [f"xn.c{kc}h", "gwl"], [pk])
            lg = U3[:, 0:16]
            mx8 = U3[:, 16:24]
            s1_ = U3[:, 24:25]
            s2_ = U3[:, 25:26]
            wk_ = U3[:, 32:48]
            vop(lambda e, _p=pt, _l=lg: e.tensor_copy(_l, _p[0:128, 0:16]), [pk], ["lg"])
            vop(lambda e, _l=lg, _m=s1_: e.reduce_max(_m, _l, axis=AX.X), ["lg"], ["lgm"])
            vop(lambda e, _m=s1_: e.tensor_scalar_mul(_m, _m, -1.0), ["lgm"], ["lgm"])
            sop(lambda e, _l=lg, _m=s1_, _s=s2_: e.activation(
                _l, _l, AF.Exp, bias=_m, scale=1.0, accum_out=_s),
                ["lg", "lgm"], ["lg", "lgs"])
            vop(lambda e, _s=s2_: e.reciprocal(_s, _s), ["lgs"], ["lgs"])
            vop(lambda e, _l=lg, _s=s2_: e.tensor_scalar_mul(_l, _l, _s), ["lg", "lgs"], ["lg"])
            vop(lambda e, _l=lg, _m=mx8: e.max(out=_m, in_=_l), ["lg"], ["mx8"])
            vop(lambda e, _m=mx8: e.memset(_m[:, 4:8], 0.0), ["mx8"], ["mx8"])
            vop(lambda e, _l=lg, _m=mx8, _w=wk_: e.match_replace(
                out=_w, in_to_replace=_m, in_values=_l, imm_value=0.0),
                ["lg", "mx8"], ["wk"])
            vop(lambda e, _m=mx8, _s=s1_: e.reduce_sum(_s, _m[:, 0:4], axis=AX.X),
                ["mx8"], ["tsum"])
            vop(lambda e, _s=s1_: e.reciprocal(_s, _s), ["tsum"], ["tsum"])
            vop(lambda e, _l=lg, _w=wk_: e.tensor_sub(_w, _l, _w), ["lg", "wk"], ["wk"])
            vop(lambda e, _w=wk_, _s=s1_, _tc=tc: e.tensor_scalar_mul(
                CN[:, 512 + _tc * 16:512 + (_tc + 1) * 16], _w, _s),
                ["wk", "tsum"], [f"cm{tc}"])

        U1B3 = U1.ap().bitcast(BF16)
        ATH = lambda hc: U1B3[:, hc * 2048:(hc + 1) * 2048]
        ATL = lambda hc: U1B3[:, 8192 + hc * 2048:8192 + (hc + 1) * 2048]
        W1F = BBB[:, 0:1024]
        W3F = BBB[:, 1024:2048]
        W1Hs = lambda p: U1B3[:, 16384 + p * 4096:16384 + p * 4096 + 1024]
        W1Ls = lambda p: U1B3[:, 17408 + p * 4096:17408 + p * 4096 + 1024]
        W3Hs = lambda p: U1B3[:, 18432 + p * 4096:18432 + p * 4096 + 1024]
        W3Ls = lambda p: U1B3[:, 19456 + p * 4096:19456 + p * 4096 + 1024]
        W2H = U1B3[:, 24576:28672]
        W2L = U1B3[:, 28672:32768]

        def split_blk(dst_h, dst_l, srcf, kf, kh_, kl_):
            vop(lambda e, _h=dst_h, _s=srcf: e.tensor_copy(_h, _s), [kf, kh_], [kh_])
            vop(lambda e, _l=dst_l, _s=srcf, _h=dst_h: e.tensor_tensor(
                _l, _s, _h, ALU.subtract), [kf, kh_, kl_], [kl_])

        def expert(ei, w1d, w3d, w2d, shared):
            for hc in range(4):
                p_ = hc % 2
                W1H, W1L = W1Hs(p_), W1Ls(p_)
                W3H, W3L = W3Hs(p_), W3Ls(p_)
                kp = str(p_)
                b.dma("sync", *colblocks(W1F, w1d[:, hc * 128:(hc + 1) * 128], 8),
                      reads=["w1f"], writes=["w1f"])
                split_blk(W1H, W1L, W1F, "w1f", "w1h" + kp, "w1l" + kp)
                b.dma("sync", *colblocks(W3F, w3d[:, hc * 128:(hc + 1) * 128], 8),
                      reads=["w3f"], writes=["w3f"])
                split_blk(W3H, W3L, W3F, "w3f", "w3h" + kp, "w3l" + kp)
                for kc in range(8):
                    for wh_, wl_, kh_, kl_, pb in (
                            (W1H, W1L, "w1h" + kp, "w1l" + kp, 0),
                            (W3H, W3L, "w3h" + kp, "w3l" + kp, 4)):
                        for tk in range(4):
                            mm(PS[pb + tk][:, :], wh_[:, kc * 128:(kc + 1) * 128],
                               xh(kc, tk * 512, (tk + 1) * 512), kc == 0, False,
                               [kh_, f"xn.c{kc}h"], [PK[pb + tk]])
                        for tk in range(4):
                            mm(PS[pb + tk][:, :], wh_[:, kc * 128:(kc + 1) * 128],
                               xl(kc, tk * 512, (tk + 1) * 512), False, False,
                               [kh_, f"xn.c{kc}l"], [PK[pb + tk]])
                        for tk in range(4):
                            mm(PS[pb + tk][:, :], wl_[:, kc * 128:(kc + 1) * 128],
                               xh(kc, tk * 512, (tk + 1) * 512), False, kc == 7,
                               [kl_, f"xn.c{kc}h"], [PK[pb + tk]])
                for tk in range(4):
                    sl = U2[:, 1024 + (tk % 2) * 512:1024 + (tk % 2 + 1) * 512]
                    slk = f"silu{tk % 2}"
                    sop(lambda e, _s=sl, _tk=tk: e.activation(
                        _s, PS[_tk][:, :], AF.Silu, bias=0.0, scale=1.0),
                        [PK[tk]], [slk])
                    pr = U2[:, 3072 + (tk % 2) * 512:3072 + (tk % 2 + 1) * 512]
                    prk = f"prod{tk % 2}"
                    vop(lambda e, _s=sl, _tk=tk, _p=pr: e.tensor_mul(
                        _p, _s, PS[4 + _tk][:, :]), [slk, PK[4 + tk]], [prk])
                    vop(lambda e, _p=pr, _hc=hc, _tk=tk: e.tensor_copy(
                        ATH(_hc)[:, _tk * 512:(_tk + 1) * 512], _p),
                        [prk], [f"act{hc}.{tk}h"])
                    vop(lambda e, _p=pr, _hc=hc, _tk=tk: e.tensor_tensor(
                        ATL(_hc)[:, _tk * 512:(_tk + 1) * 512], _p,
                        ATH(_hc)[:, _tk * 512:(_tk + 1) * 512], ALU.subtract),
                        [prk, f"act{hc}.{tk}h"], [f"act{hc}.{tk}l"])
            for hc in range(4):
                b.dma("sync", W1F, w2d[hc * 128:(hc + 1) * 128, :],
                      reads=["w1f", f"w2h{hc}", f"w2l{hc}"],
                      writes=["w1f"])
                split_blk(W2H[:, hc * 1024:(hc + 1) * 1024],
                          W2L[:, hc * 1024:(hc + 1) * 1024], W1F,
                          "w1f", f"w2h{hc}", f"w2l{hc}")
            for tc in range(16):
                for hc in range(4):
                    ah = ATH(hc)[:, tc * 128:(tc + 1) * 128]
                    al = ATL(hc)[:, tc * 128:(tc + 1) * 128]
                    ka = [f"act{hc}.{tc // 4}h", f"act{hc}.{tc // 4}l",
                          f"w2h{hc}", f"w2l{hc}"]
                    for half in range(2):
                        w2hs = W2H[:, hc * 1024 + half * 512:hc * 1024 + (half + 1) * 512]
                        mm(PS[half][:, :], ah, w2hs, hc == 0, False, ka, [PK[half]])
                    for half in range(2):
                        w2ls = W2L[:, hc * 1024 + half * 512:hc * 1024 + (half + 1) * 512]
                        mm(PS[half][:, :], ah, w2ls, False, False, ka, [PK[half]])
                    for half in range(2):
                        w2hs = W2H[:, hc * 1024 + half * 512:hc * 1024 + (half + 1) * 512]
                        mm(PS[half][:, :], al, w2hs, False, hc == 3, ka, [PK[half]])
                for half in range(2):
                    pt, pk = PS[half], PK[half]
                    stg = U2[:, half * 512:(half + 1) * 512]
                    sk = f"w2s{half}"
                    if shared:
                        vop(lambda e, _s=stg, _p=pt: e.tensor_copy(_s, _p[:, :]),
                            [pk], [sk])
                    else:
                        vop(lambda e, _s=stg, _p=pt, _tc=tc, _e=ei: e.tensor_scalar_mul(
                            _s, _p[:, :], CN[:, 512 + _tc * 16 + _e:513 + _tc * 16 + _e]),
                            [pk, f"cm{tc}"], [sk])
                    b.dma("gpsimd",
                          g_d[tc * 128:(tc + 1) * 128, half * 512:(half + 1) * 512],
                          stg, reads=[sk], writes=[f"gacc{tc}.{half}"],
                          accum=(ei > 0))

        for ei in range(NE):
            expert(ei, w1Th[ei], w3Th[ei], w2Th[ei], False)
        expert(NE, sw1T, sw3T, sw2T, True)
        b.barrier()

        if UPTO < 5:
            _sel = os.environ.get("K_TAPSEL", "q")
            if _sel == "q":
                for dc in range(8):
                    tap(q_sp[dc * 128:(dc + 1) * 128, :], ["qsp0.0"])
            elif _sel == "g":
                for tc in range(16):
                    tap(g_d[tc * 128:(tc + 1) * 128, :],
                        [f"gacc{tc}.0", f"gacc{tc}.1"])
            elif _sel == "cm":
                tap(CN[0:128, 512:768], [f"cm{t}" for t in range(16)])
            elif _sel == "kv":
                for dc in range(8):
                    tap(k_loc[dc * 128:(dc + 1) * 128, :], ["kd7.3"])
            elif _sel == "vt":
                for tc in range(16):
                    tap(vt_loc[tc * 128:(tc + 1) * 128, :], ["vtd7.3.3"])
        _ck(5)
        for s4 in range(4):
            b.dma("sync", CN[0:1, 352 + s4 * 16:368 + s4 * 16], kn_gv(s4),
                  reads=["kn_g"], writes=["kng"])
        vop(lambda e: e.tensor_max(CN[0:1, 352:368], CN[0:1, 352:368], CN[0:1, 368:384]),
            ["kng"], ["kng"])
        vop(lambda e: e.tensor_max(CN[0:1, 384:400], CN[0:1, 384:400], CN[0:1, 400:416]),
            ["kng"], ["kngb"])
        vop(lambda e: e.tensor_max(CN[0:1, 352:368], CN[0:1, 352:368], CN[0:1, 384:400]),
            ["kng", "kngb"], ["kmx"])
        vop(lambda e: e.tensor_mul(CN[0:1, 416:432], CN[0:1, 320:336], CN[0:1, 352:368]),
            ["kmx"], ["cr"])
        vop(lambda e: e.tensor_scalar_mul(CN[0:1, 416:432], CN[0:1, 416:432], -0.125),
            ["cr"], ["cr"])
        mm(PS[4][:, 0:16], CN[0:1, 0:128], CN[0:1, 416:432], True, True,
           ["cr"], [PK[4]])
        vop(lambda e: e.tensor_copy(CN[:, 432:448], PS[4][:, 0:16]), [PK[4]], ["negc"])

        U1B5 = U1.ap().bitcast(BF16)
        QHh = U1B5[0:64, 0:2048]
        QHl = U1B5[0:64, 2048:4096]
        KTh = lambda j: U1B5[0:64, 4096 + j * 512:4096 + (j + 1) * 512]
        KTlo = lambda j: U1B5[0:64, 5120 + j * 512:5120 + (j + 1) * 512]
        FR = lambda j: U1[:, 3072 + j * 512:3072 + (j + 1) * 512]
        PTg = lambda kc4, qt: U1[:, 4096 + (kc4 * 4 + qt) * 512:
                                 4096 + (kc4 * 4 + qt + 1) * 512]
        RR = U1[0:1, 12288:12800]
        ANS = U1[:, 12800:13312]
        for j in range(2):
            for kc4 in range(4):
                vop(lambda e, _j=j, _k=kc4: e.memset(
                    FR(_j)[:, _k * 128 + 64:_k * 128 + 65], 1.0), [], [f"fri{j}.{kc4}"])
                vop(lambda e, _j=j, _k=kc4: e.memset(
                    FR(_j)[:, _k * 128 + 65:_k * 128 + 128], 0.0), [], [f"frz{j}.{kc4}"])

        for h in range(NH):
            b.dma("sync", QHh, q_spb[0, h * 64:(h + 1) * 64, :],
                  reads=["qh"], writes=["qh"])
            b.dma("sync", QHl, q_spb[1, h * 64:(h + 1) * 64, :],
                  reads=["qh"], writes=["qh"])
            for kvb in range(16):
                j = kvb % 2
                src = (kvb * 512) // 2048
                col = (kvb * 512) % 2048
                b.dma("sync", KTh(j), k_gvb(src)[0, h * 64:(h + 1) * 64, col:col + 512],
                      reads=[f"kt{j}", "k_g"], writes=[f"kt{j}"])
                b.dma("sync", KTlo(j), k_gvb(src)[1, h * 64:(h + 1) * 64, col:col + 512],
                      reads=[f"kt{j}", "k_g"], writes=[f"kt{j}"])
                b.dma("sync",
                      FR(j).rearrange("p (kc c) -> p kc c", kc=4)[:, :, 0:64],
                      vt_gv(src)[col:col + 512, h * 64:(h + 1) * 64].rearrange(
                          "(kc p) c -> p kc c", p=128),
                      reads=[f"fr{j}", "vt_g"], writes=[f"fr{j}"])
                for kc4 in range(4):
                    for qt in range(4):
                        pss, psk = PS[4 + qt], PK[4 + qt]
                        mm(pss[:, :], KTh(j)[:, kc4 * 128:(kc4 + 1) * 128],
                           QHh[:, qt * 512:(qt + 1) * 512], True, False,
                           [f"kt{j}", "qh"], [psk])
                        mm(pss[:, :], KTh(j)[:, kc4 * 128:(kc4 + 1) * 128],
                           QHl[:, qt * 512:(qt + 1) * 512], False, False,
                           [f"kt{j}", "qh"], [psk])
                    for qt in range(4):
                        pss, psk = PS[4 + qt], PK[4 + qt]
                        mm(pss[:, :], KTlo(j)[:, kc4 * 128:(kc4 + 1) * 128],
                           QHh[:, qt * 512:(qt + 1) * 512], False, True,
                           [f"kt{j}", "qh"], [psk])
                        sop(lambda e, _p=pss, _h=h, _o=PTg(kc4, qt): e.activation(
                            _o, _p[:, :], AF.Exp, bias=CN[:, 432 + _h:433 + _h],
                            scale=0.125), [psk, "negc"], [f"pt{kc4}.{qt}"])
                for qt in range(4):
                    for kc4 in range(4):
                        for qs in range(4):
                            qg = qt * 4 + qs
                            bank, off = qg // 7, (qg % 7) * 65
                            mm(PS[bank][:, off:off + 65],
                               PTg(kc4, qt)[:, qs * 128:(qs + 1) * 128],
                               FR(j)[:, kc4 * 128:kc4 * 128 + 65],
                               (kvb == 0 and kc4 == 0 and qg % 7 == 0),
                               (kvb == 15 and kc4 == 3),
                               [f"fr{j}", f"fri{j}.{kc4}", f"frz{j}.{kc4}",
                                f"pt{kc4}.{qt}"], [PK[bank]])
            ATS = U1[:, 13312:13376]
            RR1c = U1[:, 13376:13377]
            AOD = U1[:, 13440:13568]
            for qg in range(16):
                bank, off = qg // 7, (qg % 7) * 65
                vop(lambda e, _b=bank, _o=off: e.tensor_copy(
                    RR1c, PS[_b][:, _o + 64:_o + 65]), [PK[bank]], ["rrc"])
                vop(lambda e: e.reciprocal(RR1c, RR1c), ["rrc"], ["rrc"])
                vop(lambda e, _b=bank, _o=off: e.tensor_scalar_mul(
                    ATS, PS[_b][:, _o:_o + 64], RR1c), [PK[bank], "rrc"], ["ats"])
                mm(PS[3][0:64, 0:128], ATS, IDn[:, :], True, True,
                   ["ats", "IDn"], [PK[3]])
                if h % 2 == 1:
                    vop(lambda e, _h=h, _qg=qg: e.tensor_copy(
                        u0c(_h // 2, _qg * 128, (_qg + 1) * 128)[64:128, :],
                        PS[3][64:128, 0:128] if False else PS[3][0:64, 0:128]),
                        [PK[3]], ["aodx"]) if False else None
                    vop(lambda e, _qg=qg: e.tensor_copy(
                        AOD[0:64, :], PS[3][0:64, 0:128]), [PK[3]], ["aod"])
                    b.dma("scalar",
                          u0c(h // 2, qg * 128, (qg + 1) * 128)[64:128, :],
                          AOD[0:64, :], reads=["aod"], writes=[f"a.{h // 2}.hi"])
                else:
                    vop(lambda e, _h=h, _qg=qg: e.tensor_copy(
                        u0c(_h // 2, _qg * 128, (_qg + 1) * 128)[0:64, :],
                        PS[3][0:64, 0:128]), [PK[3]], [f"a.{h // 2}.lo"])
        b.barrier()

        if UPTO < 6:
            for c in range(8):
                tap(u0c(c, 0, T), [f"a.c{c}" if False else f"a.{c}.lo", f"a.{c}.hi"])
        _ck(6)
        b.dma("sync", CN[0:1, 264:264 + D] if False else BBB[0:1, 0:D], bout_h,
              writes=["boutr"])
        for half in range(2):
            mm(PS[6][:, 0:512], CN[0:1, 0:128], BBB[0:1, half * 512:(half + 1) * 512],
               True, True, ["boutr"], [PK[6]])
            vop(lambda e, _h=half: e.tensor_copy(
                BBB[:, 1024 + _h * 512:1024 + (_h + 1) * 512], PS[6][:, 0:512]),
                [PK[6], "boutr"], [f"boutb{half}"])
        for half in range(2):
            wt = U1[:, 8192 + half * 4096: 8192 + (half + 1) * 4096]
            wk = f"wo{half}"
            b.dma("sync", *colblocks(wt, woutT[:, half * 512:(half + 1) * 512], 8),
                  writes=[wk])
            for tc in range(16):
                pt, pk = PS[tc % 2], PK[tc % 2]
                for kc in range(8):
                    mm(pt[:, :], u0c(kc, tc * 128, (tc + 1) * 128),
                       wt[:, kc * 512:(kc + 1) * 512], kc == 0, kc == 7, [wk], [pk])
                stg = U3[:, 0:512]
                vop(lambda e, _p=pt, _s=stg, _h=half: e.tensor_add(
                    _s, _p[:, :], BBB[:, 1024 + _h * 512:1024 + (_h + 1) * 512]),
                    [pk, f"boutb{half}"], ["sstg"])
                b.dma("gpsimd", s_d[tc * 128:(tc + 1) * 128, half * 512:(half + 1) * 512],
                      stg, reads=["sstg"], writes=[f"sd{tc}.{half}"])
        b.barrier()

        if UPTO < 7:
            for tc in range(16):
                tap(s_d[tc * 128:(tc + 1) * 128, :], [f"sd{tc}.0", f"sd{tc}.1"])
        _ck(7)
        for tc in range(16):
            xt = U2[:, 0:1024]
            st_ = U2[:, 1024:2048]
            gt = U2[:, 2048:3072]
            b.dma("sync", xt, x_h[tc * 128:(tc + 1) * 128, :], reads=["p7x"], writes=["p7x"])
            b.dma("sync", st_, s_d[tc * 128:(tc + 1) * 128, :], reads=["p7s"], writes=["p7s"])
            b.dma("sync", gt, g_d[tc * 128:(tc + 1) * 128, :], reads=["p7g"], writes=["p7g"])
            sop(lambda e, _g=gt: e.activation(_g, _g, AF.Silu, bias=0.0, scale=1.0),
                ["p7g"], ["p7g"])
            vop(lambda e, _s=st_, _g=gt: e.tensor_mul(_s, _s, _g), ["p7s", "p7g"], ["p7s"])
            vop(lambda e, _x=xt, _s=st_, _tc=tc: e.tensor_add(
                U1[:, _tc * 1024:(_tc + 1) * 1024], _x, _s),
                ["p7x", "p7s"], [f"x1.{tc}"])
        b.barrier()

        if UPTO < 8:
            for tc in range(16):
                tap(U1[:, tc * 1024:(tc + 1) * 1024], [f"x1.{tc}"])
        _ck(8)
        def load_x1(tt, xt, xk):
            vop(lambda e, _x=xt, _tt=tt: e.tensor_copy(_x, U1[:, _tt * 1024:(_tt + 1) * 1024]),
                [xk, f"x1.{tt}"], [xk])

        ln_16tiles(load_x1,
                   lambda c, tt: (xh(c, tt * 128, (tt + 1) * 128),
                                  xl(c, tt * 128, (tt + 1) * 128), f"xn1.c{c}"),
                   1e-5, (0, 1), amax_slot=CN[:, 960:961])
        for tc in range(16):
            b.dma("scalar", x1_d[tc * 128:(tc + 1) * 128, :],
                  U1[:, tc * 1024:(tc + 1) * 1024], reads=[f"x1.{tc}"],
                  writes=[f"x1d{tc}"])
        _ck(7.3)
        mm(PS[5][0:1, 0:128], CN[:, 960:961], IDn[:, :], True, True,
           ["runamax", "IDn"], [PK[5]])
        vop(lambda e: e.tensor_reduce(CN[0:1, 961:962], PS[5][0:1, 0:128],
                                      axis=AX.X, op=ALU.max), [PK[5]], ["am1"])
        b.dma("scalar", am_l, CN[0:1, 961:962], reads=["am1"], writes=["amld"])
        b.add("gpsimd", lambda e: e.collective_compute(
            "AllReduce", ALU.max, replica_groups=ALL8, ins=[am_l], outs=[am_gd]),
            reads=["amld"], writes=["amg"])
        _ck(7.6)
        WQF = U2[:, 0:1024]
        U2B8 = U2.ap().bitcast(BF16)
        WQH8 = U2B8[:, 2048:3072]
        WQL8 = U2B8[:, 3072:4096]
        for dc in range(24):
            b.dma("sync", *colblocks(WQF, wqkv[:, dc * 128:(dc + 1) * 128], 8),
                  reads=["wqf", "wqh8", "wql8"], writes=["wqf"])
            vop(lambda e: e.tensor_copy(WQH8[:, :], WQF), ["wqf", "wqh8"], ["wqh8"])
            vop(lambda e: e.tensor_tensor(WQL8[:, :], WQF, WQH8[:, :], ALU.subtract),
                ["wqf", "wqh8", "wql8"], ["wql8"])
            _wq = os.environ.get("K_WQDBG", "")
            for s, wv, kk in ((0, WQH8, "wqh8"), (1, WQL8, "wql8")):
                if _wq == "nodma":
                    vop(lambda e: e.memset(CN[:, 1022:1023], 0.0),
                        [kk], [f"wqkvp{s}.{dc}"])
                elif _wq == "f32dst":
                    b.dma("gpsimd", wqkvp[s, dc], U2[:, 0:512],
                          reads=[kk, "wqf"], writes=[f"wqkvp{s}.{dc}"])
                else:
                    b.dma("scalar", wqkvpb[s, dc], wv,
                          reads=[kk], writes=[f"wqkvp{s}.{dc}"])
        b.barrier()

        if UPTO < 9:
            for c in range(8):
                tap(u0c(c, 0, T), [f"xn1.c{c}"])
        _ck(9)
        QKVg = lambda dc: U1[:, 2048 + dc * 256:2048 + (dc + 1) * 256]
        FR1 = lambda h, kc2: U1[:, 8192 + (h * 2 + kc2) * 128:8192 + (h * 2 + kc2 + 1) * 128]
        QHs = U1[0:64, 12288:12544]
        KHs = U1[0:64, 12544:12800]
        PPl = lambda kc2: U1[:, 12800 + kc2 * 256:12800 + (kc2 + 1) * 256]
        RR1 = U1[0:1, 13312:13568]
        ANS1 = U1[:, 13568:13824]
        for h in range(NH):
            for kc2 in range(2):
                vop(lambda e, _h=h, _k=kc2: e.memset(FR1(_h, _k)[:, 0:1], 1.0),
                    [], [f"f1i{h}.{kc2}"])
                vop(lambda e, _h=h, _k=kc2: e.memset(FR1(_h, _k)[:, 1:64], 0.0),
                    [], [f"f1z{h}.{kc2}"])

        for g in range(NG):
            g0 = g * GS
            U1B9 = U1.ap().bitcast(BF16)
            WP9H = U1B9[:, 0:1024]
            WP9L = U1B9[:, 1024:2048]
            for dc in range(24):
                b.dma("sync", WP9H, wqkvpb[0, dc],
                      reads=["wqk0", "wqk1", f"wqkvp0.{dc}"], writes=["wqk0"])
                b.dma("sync", WP9L, wqkvpb[1, dc],
                      reads=["wqk0", "wqk1", f"wqkvp1.{dc}"], writes=["wqk1"])
                pt, pk = PS[4 + (dc % 2)], PK[4 + (dc % 2)]
                for kc in range(8):
                    mm(pt[:, 0:256], WP9H[:, kc * 128:(kc + 1) * 128],
                       xh(kc, g0, g0 + 256), kc == 0, False,
                       ["wqk0", f"xn1.c{kc}h"], [pk])
                    mm(pt[:, 0:256], WP9H[:, kc * 128:(kc + 1) * 128],
                       xl(kc, g0, g0 + 256), False, False,
                       ["wqk0", f"xn1.c{kc}l"], [pk])
                    mm(pt[:, 0:256], WP9L[:, kc * 128:(kc + 1) * 128],
                       xh(kc, g0, g0 + 256), False, kc == 7,
                       ["wqk1", f"xn1.c{kc}h"], [pk])
                vop(lambda e, _p=pt, _dc=dc: e.tensor_copy(QKVg(_dc), _p[:, 0:256]),
                    [pk], [f"kv{dc}"])
            for which, base in (("q", 0), ("k", 8)):
                sq = U3[:, 0:256]
                for kc in range(8):
                    vop(lambda e, _b=base, _kc=kc, _s=sq: e.tensor_mul(
                        _s, QKVg(_b + _kc), QKVg(_b + _kc)), [f"kv{base + kc}"], ["sqg"])
                    mm(PS[6][0:1, 0:256], CN[:, 448:449], sq, kc == 0, kc == 7,
                       ["sqg", "onescol"], [PK[6]])
                fr_ = U4[0:1, 0:256]
                vop(lambda e, _f=fr_: e.tensor_scalar(
                    _f, PS[6][0:1, 0:256], 1.0 / D, 1e-6, ALU.mult, ALU.add),
                    [PK[6]], ["f1pre"])
                vop(lambda e, _f=fr_: e.reciprocal(_f, _f), ["f1pre"], ["f1r0"])
                sop(lambda e, _f=fr_: e.activation(_f, _f, AF.Sqrt, bias=0.0,
                                                   scale=1.0), ["f1r0"], ["f1row"])
                mm(PS[6][:, 0:256], CN[0:1, 0:128], fr_, True, True,
                   ["f1row"], [PK[6]])
                for kc in range(8):
                    if which == "q":
                        vop(lambda e, _kc=kc: e.tensor_mul(
                            QKVg(_kc), QKVg(_kc), PS[6][:, 0:256]),
                            [f"kv{kc}", PK[6]], [f"kv{kc}"])
                    else:
                        vop(lambda e, _kc=kc: e.scalar_tensor_tensor(
                            QKVg(8 + _kc), QKVg(8 + _kc),
                            CN[:, 272 + _kc:273 + _kc], PS[6][:, 0:256],
                            ALU.mult, ALU.mult),
                            [f"kv{8 + kc}", "lnw2", PK[6]], [f"kv{8 + kc}"])
            for which, base, col in (("q", 0, 464), ("k", 8, 480)):
                sq = U3[:, 0:256]
                for kc in range(8):
                    vop(lambda e, _b=base, _kc=kc, _s=sq: e.tensor_mul(
                        _s, QKVg(_b + _kc), QKVg(_b + _kc)), [f"kv{base + kc}"], ["sqg"])
                    mm(PS[6][0:16, 0:256], CN[:, 128 + kc * 16:128 + (kc + 1) * 16], sq,
                       kc == 0, kc == 7, ["sqg", "bd"], [PK[6]])
                vop(lambda e, _c=col: e.tensor_reduce(
                    CN[0:16, _c:_c + 1], PS[6][0:16, 0:256], axis=AX.X, op=ALU.max),
                    [PK[6]], [f"n1{which}"])
            vop(lambda e: e.tensor_mul(CN[0:16, 464:465], CN[0:16, 464:465],
                                       CN[0:16, 480:481]), ["n1q", "n1k"], ["n1c"])
            sop(lambda e: e.activation(CN[0:16, 465:466], CN[0:16, 464:465],
                                       AF.Sqrt, bias=0.0, scale=1.0), ["n1c"], ["n1s"])
            vop(lambda e: e.tensor_scalar_mul(CN[0:16, 465:466], CN[0:16, 465:466],
                                              -0.125), ["n1s"], ["n1s"])
            mm(PS[6][0:1, 0:16], CN[0:16, 465:466], IDn[0:16, 0:16], True, True,
               ["n1s", "IDn"], [PK[6]])
            vop(lambda e: e.tensor_copy(U3[0:1, 496:512], PS[6][0:1, 0:16]),
                [PK[6]], ["n1r"])
            mm(PS[6][:, 0:16], CN[0:1, 0:128], U3[0:1, 496:512], True, True,
               ["n1r"], [PK[6]])
            vop(lambda e: e.tensor_copy(CN[:, 496:512], PS[6][:, 0:16]),
                [PK[6]], ["negc1"])
            for dc in range(8):
                for ts in range(2):
                    mm(PS[6][:, 0:128], QKVg(16 + dc)[:, ts * 128:(ts + 1) * 128],
                       IDn[:, :], True, True, [f"kv{16 + dc}", "IDn"], [PK[6]])
                    h0, h1 = 2 * dc, 2 * dc + 1
                    vop(lambda e, _h=h0, _t=ts: e.tensor_copy(
                        FR1(_h, _t)[:, 64:128], PS[6][:, 0:64]),
                        [PK[6]], [f"f1d{h0}.{ts}"])
                    vop(lambda e, _h=h1, _t=ts: e.tensor_copy(
                        FR1(_h, _t)[:, 64:128], PS[6][:, 64:128]),
                        [PK[6]], [f"f1d{h1}.{ts}"])
            for h in range(NH):
                ch, ro = h // 2, (h % 2) * 64
                b.dma("sync", QHs, QKVg(ch)[ro:ro + 64, :],
                      reads=["qhs", f"kv{ch}"], writes=["qhs"])
                b.dma("sync", KHs, QKVg(8 + ch)[ro:ro + 64, :],
                      reads=["khs", f"kv{8 + ch}"], writes=["khs"])
                for kc2 in range(2):
                    mm(PS[6][:, 0:256], KHs[:, kc2 * 128:(kc2 + 1) * 128], QHs,
                       True, True, ["qhs", "khs"], [PK[6]])
                    sop(lambda e, _h=h, _k=kc2: e.activation(
                        PPl(_k), PS[6][:, 0:256], AF.Exp,
                        bias=CN[:, 496 + _h:497 + _h], scale=0.125),
                        [PK[6], "negc1"], [f"pp{kc2}"])
                    vop(lambda e, _k=kc2: e.tensor_mul(
                        PPl(_k), PPl(_k), MK[:, _k * 256:(_k + 1) * 256]),
                        [f"pp{kc2}", "MK"], [f"pp{kc2}"])
                    mm(PS[7][:, 0:256], FR1(h, kc2), PPl(kc2), kc2 == 0, kc2 == 1,
                       [f"pp{kc2}", f"f1d{h}.{kc2}", f"f1i{h}.{kc2}", f"f1z{h}.{kc2}"],
                       [PK[7]])
                vop(lambda e: e.tensor_copy(RR1, PS[7][0:1, 0:256]), [PK[7]], ["rr1"])
                vop(lambda e: e.reciprocal(RR1, RR1), ["rr1"], ["rr1"])
                mm(PS[5][:, 0:256], CN[0:1, 0:128], RR1, True, True, ["rr1"], [PK[5]])
                RB1 = U1[:, 13824:14080]
                vop(lambda e: e.tensor_copy(RB1, PS[5][:, 0:256]), [PK[5]], ["rb1"])
                if h % 2 == 1:
                    vop(lambda e, _ch=ch: e.tensor_mul(
                        QKVg(16 + _ch)[64:128, :], PS[7][64:128, 0:256],
                        RB1[64:128, :]),
                        [PK[7], "rb1", f"kv{16 + ch}"], [f"a1.{ch}.hi", f"kv{16 + ch}"])
                else:
                    vop(lambda e: e.tensor_mul(ANS1[64:128, :], PS[7][64:128, 0:256],
                                               RB1[64:128, :]),
                        [PK[7], "rb1"], ["ans1"])
                    b.dma("scalar", QKVg(16 + ch)[0:64, :], ANS1[64:128, :],
                          reads=["ans1", f"kv{16 + ch}"],
                          writes=[f"a1.{ch}.lo", f"kv{16 + ch}"])
            for q4 in range(4):
                wt = U1[:, 0:2048]
                b.dma("sync", *colblocks(wt, wproj[:, q4 * 256:(q4 + 1) * 256], 8),
                      reads=["wqk0", "wqk1"], writes=["wqk0", "wqk1"])
                for tc2 in range(2):
                    pt, pk = PS[tc2 % 2], PK[tc2 % 2]
                    for kc in range(8):
                        mm(pt[:, 0:256], QKVg(16 + kc)[:, tc2 * 128:(tc2 + 1) * 128],
                           wt[:, kc * 256:(kc + 1) * 256], kc == 0, kc == 7,
                           [f"a1.{kc}.lo", f"a1.{kc}.hi", "wqk0", "wqk1"], [pk])
                    stg = U3[:, 0:256]
                    vop(lambda e, _p=pt, _s=stg: e.tensor_copy(_s, _p[:, 0:256]),
                        [pk], ["s1s"])
                    b.dma("scalar",
                          s1_d[g0 + tc2 * 128:g0 + (tc2 + 1) * 128,
                               q4 * 256:(q4 + 1) * 256],
                          stg, reads=["s1s"], writes=[f"s1d{g}.{tc2}.{q4}"])
        b.barrier()

        if UPTO < 10:
            _sel = os.environ.get("K_TAPSEL", "s1")
            if _sel == "s1":
                for tc in range(16):
                    tap(s1_d[tc * 128:(tc + 1) * 128, :], ["s1d0.0.0"])
            elif _sel == "a1g":
                for dc in range(8):
                    tap(QKVg(16 + dc), [f"a1.{dc}.lo", f"a1.{dc}.hi"])
            elif _sel == "q1g":
                for dc in range(8):
                    tap(QKVg(dc), [f"kv{dc}"])
            elif _sel == "k1g":
                for dc in range(8):
                    tap(QKVg(8 + dc), [f"kv{8 + dc}"])
        _ck(10)
        b.dma("sync", BBB[0:1, 0:D], fbias_h, writes=["fbr"])
        for half in range(2):
            mm(PS[6][:, 0:512], CN[0:1, 0:128], BBB[0:1, half * 512:(half + 1) * 512],
               True, True, ["fbr"], [PK[6]])
            vop(lambda e, _h=half: e.tensor_copy(
                BBB[:, 1024 + _h * 512:1024 + (_h + 1) * 512], PS[6][:, 0:512]),
                [PK[6], "fbr"], [f"fbb{half}"])
        for kc in range(8):
            wt = U2[:, 0:1024]
            b.dma("sync", wt, fWT[kc * 128:(kc + 1) * 128, :], reads=["fwt"],
                  writes=["fwt"])
            vop(lambda e: e.tensor_reduce(U3[:, 0:1], U2[:, 0:1024], axis=AX.X,
                                          op=ALU.max, apply_absolute_value=True),
                ["fwt"], ["wam1"])
            if kc == 0:
                vop(lambda e: e.tensor_copy(U3[:, 1:2], U3[:, 0:1]), ["wam1"], ["wam"])
            else:
                vop(lambda e: e.tensor_max(U3[:, 1:2], U3[:, 1:2], U3[:, 0:1]),
                    ["wam1", "wam"], ["wam"])
        mm(PS[5][0:1, 0:128], U3[:, 1:2], IDn[:, :], True, True, ["wam", "IDn"], [PK[5]])
        vop(lambda e: e.tensor_reduce(U3[0:1, 2:3], PS[5][0:1, 0:128], axis=AX.X,
                                      op=ALU.max), [PK[5]], ["wamx"])

        def scale_chain(am, sc, dq, k0, k1, k2):
            vop(lambda e, _a=am: e.tensor_scalar_max(_a, _a, 1e-12), [k0], [k0])
            vop(lambda e, _a=am, _s=sc: e.reciprocal(_s, _a), [k0], [k1])
            vop(lambda e, _s=sc: e.tensor_scalar_mul(_s, _s, 448.0), [k1], [k1])
            vop(lambda e, _s=sc: e.tensor_scalar_min(_s, _s, 448.0), [k1], [k1])
            vop(lambda e, _s=sc, _d=dq: e.reciprocal(_d, _s), [k1], [k2])

        scale_chain(U3[0:1, 2:3], U3[0:1, 3:4], U3[0:1, 4:5], "wamx", "wsc", "wdq")

        def bc1(src, srck, dst, dstk, psi=5):
            mm(PS[psi][:, 0:1], CN[0:1, 0:128], src, True, True, [srck], [PK[psi]])
            vop(lambda e, _d=dst: e.tensor_copy(_d, PS[psi][:, 0:1]), [PK[psi]], [dstk])

        vop(lambda e: e.tensor_scalar_mul(U3[0:1, 5:6], U3[0:1, 3:4], 0.5),
            ["wsc"], ["wsch"])
        bc1(U3[0:1, 5:6], "wsch", U3[:, 6:7], "wschb")
        bc1(U3[0:1, 3:4], "wsc", U3[:, 7:8], "wscb")

        def quant(src, srck, dst, dstk, n, shb, shbk, scb, scbk, uscr, uscrk):
            vop(lambda e, _s=src, _h=shb, _n=n: e.tensor_scalar(
                F8s[:, 0:_n], _s, _h, None, ALU.mult), [srck, shbk], ["qc1"])
            sop(lambda e, _n=n: e.activation(XQB[:, 10240:10240 + _n], F8s[:, 0:_n],
                                             AF.Identity, bias=0.0, scale=2.0),
                ["qc1"], ["qc1b"])
            vop(lambda e, _s=src, _c=scb, _u=uscr, _n=n: e.tensor_scalar(
                _u, _s, _c, 16384.0, ALU.mult, ALU.add), [srck, scbk], [uscrk])
            vop(lambda e, _u=uscr, _n=n: e.tensor_scalar_add(
                XQB[:, 11264:11264 + _n], _u, -16384.0), [uscrk], ["qu"])
            sop(lambda e, _u=uscr, _n=n: e.activation(
                _u, XQB[:, 11264:11264 + _n], AF.Square, bias=0.0, scale=1.0),
                ["qu"], [uscrk + "s"])
            vop(lambda e, _u=uscr, _n=n: e.tensor_scalar(
                M8[:, 0:_n], _u, 2.0 ** -12, None, ALU.is_ge), [uscrk + "s"], ["qm"])
            vop(lambda e, _d=dst, _n=n: e.select(
                _d, M8[:, 0:_n], XQB[:, 10240:10240 + _n], XQB[:, 11264:11264 + _n]),
                ["qm", "qc1b", "qu"], [dstk])

        for kc in range(8):
            wt = U2[:, 0:1024]
            b.dma("sync", wt, fWT[kc * 128:(kc + 1) * 128, :], reads=["fwt"],
                  writes=["fwt"])
            quant(wt, "fwt", XQB[:, 2048 + kc * 1024:2048 + (kc + 1) * 1024],
                  f"wq{kc}", 1024, U3[:, 6:7], "wschb", U3[:, 7:8], "wscb",
                  U2[:, 1024:2048], "quw")
        b.dma("sync", U3[0:1, 8:9], am_gd, reads=["amg"], writes=["axg"])
        scale_chain(U3[0:1, 8:9], U3[0:1, 9:10], U3[0:1, 10:11], "axg", "xsc", "xdq")
        vop(lambda e: e.tensor_scalar_mul(U3[0:1, 11:12], U3[0:1, 9:10], 0.5),
            ["xsc"], ["xsch"])
        bc1(U3[0:1, 11:12], "xsch", U3[:, 12:13], "xschb")
        bc1(U3[0:1, 9:10], "xsc", U3[:, 13:14], "xscb")
        vop(lambda e: e.tensor_mul(U3[0:1, 14:15], U3[0:1, 10:11], U3[0:1, 4:5]),
            ["xdq", "wdq"], ["dqf"])
        bc1(U3[0:1, 14:15], "dqf", U3[:, 15:16], "dqfb")
        if UPTO < 10.5:
            tap(U3[0:128, 0:16], ["dqfb", "xschb", "xscb", "wschb", "wscb"])
        _ck(10.5)
        b.barrier()
        for tc in range(16):
            for kc in range(8):
                vrec = U4[:, 128:256]
                vop(lambda e, _v=vrec, _kc=kc, _tc=tc: e.tensor_add(
                    _v, xh(_kc, _tc * 128, (_tc + 1) * 128),
                    xl(_kc, _tc * 128, (_tc + 1) * 128)),
                    ["vrec"], ["vrec"])
                quant(vrec, "vrec",
                      XQB[:, kc * 128:(kc + 1) * 128], f"xq{kc}", 128,
                      U3[:, 12:13], "xschb", U3[:, 13:14], "xscb",
                      U4[:, 0:128], "qux")
            for half in range(2):
                pt, pk = PS[(tc * 2 + half) % 2], PK[(tc * 2 + half) % 2]
                for kc in range(8):
                    mm(pt[:, :], XQB[:, kc * 128:(kc + 1) * 128],
                       XQB[:, 2048 + kc * 1024 + half * 512:
                           2048 + kc * 1024 + (half + 1) * 512],
                       kc == 0, kc == 7, [f"xq{kc}", f"wq{kc}"], [pk])
                ob = XQB[:, 1024 + half * 512:1024 + (half + 1) * 512]
                vop(lambda e, _p=pt, _h=half, _o=ob: e.scalar_tensor_tensor(
                    _o, _p[:, :], U3[:, 15:16], BBB[:, 1024 + _h * 512:1024 + (_h + 1) * 512],
                    ALU.mult, ALU.add), [pk, "dqfb", f"fbb{half}"], ["l1o"])
                gt = U2[:, 1024 + half * 512:1024 + (half + 1) * 512]
                sop(lambda e, _o=ob, _g=gt: e.activation(_g, _o, AF.Silu, bias=0.0,
                                                         scale=1.0), ["l1o"], ["l1g"])
                s1t = U2[:, 2048 + half * 512:2048 + (half + 1) * 512]
                b.dma("sync", s1t,
                      s1_d[tc * 128:(tc + 1) * 128, half * 512:(half + 1) * 512],
                      reads=["l1s1"], writes=["l1s1"])
                x1t = U2[:, 3072 + half * 512:3072 + (half + 1) * 512]
                b.dma("sync", x1t,
                      x1_d[tc * 128:(tc + 1) * 128, half * 512:(half + 1) * 512],
                      reads=["l1x1"], writes=["l1x1"])
                vop(lambda e, _s=s1t, _g=gt: e.tensor_mul(_s, _s, _g),
                    ["l1s1", "l1g"], ["l1sg"])
                ot = U2[:, half * 512:(half + 1) * 512]
                _dbg = os.environ.get("K_OUTSEL", "")
                if _dbg == "xq":
                    sop(lambda e, _o=ot, _h=half: e.activation(
                        _o, XQB[:, _h * 512:(_h + 1) * 512], AF.Identity,
                        bias=0.0, scale=1.0),
                        [f"xq{k}" for k in range(8)], ["ott"])
                elif _dbg == "g1":
                    vop(lambda e, _o=ot, _g=gt: e.tensor_copy(_o, _g),
                        ["l1g"], ["ott"])
                elif _dbg == "s1":
                    vop(lambda e, _o=ot, _s=s1t: e.tensor_copy(_o, _s),
                        ["l1sg"], ["ott"])
                else:
                    vop(lambda e, _o=ot, _s=s1t, _x=x1t: e.tensor_add(_o, _x, _s),
                        ["l1sg", "l1x1"], ["ott"])
                b.dma("gpsimd",
                      out_h[tc * 128:(tc + 1) * 128, half * 512:(half + 1) * 512],
                      ot, reads=["ott"], writes=["outd"])
        b.barrier()

    except _Done:
        b.barrier()

    block = e_(nc.Block())
    b.junk = (junk_v, junk_s, junk_g)
    b.finalize(nc, block)
    ctx.close()
    return nc


def kernel(**inputs):
    f32 = np.float32
    x = np.ascontiguousarray(np.asarray(inputs["L0_attn_x"]).reshape(16384, 1024)).astype(f32, copy=False)
    tr = lambda a: np.ascontiguousarray(np.asarray(a).astype(f32, copy=False).T)
    common = {
        "wqT": tr(inputs["L0_attn_Wq"]), "wkT": tr(inputs["L0_attn_Wk"]),
        "wvT": tr(inputs["L0_attn_Wv"]), "woutT": tr(inputs["L0_attn_Wout"]),
        "bout": np.asarray(inputs["L0_attn_bout"], f32).reshape(1, D),
        "gate_wT": tr(inputs["L0_ffn_gate_w"]),
        "w1T": np.ascontiguousarray(np.asarray(inputs["L0_ffn_w1"], f32).transpose(0, 2, 1)),
        "w3T": np.ascontiguousarray(np.asarray(inputs["L0_ffn_w3"], f32).transpose(0, 2, 1)),
        "w2T": np.ascontiguousarray(np.asarray(inputs["L0_ffn_w2"], f32).transpose(0, 2, 1)),
        "sw1T": tr(inputs["L0_ffn_sw1"]), "sw3T": tr(inputs["L0_ffn_sw3"]),
        "sw2T": tr(inputs["L0_ffn_sw2"]),
        "wqkv": np.ascontiguousarray(np.asarray(inputs["L1_attn_w_qkv"], f32)),
        "ln_w": np.asarray(inputs["L1_attn_ln_w"], f32).reshape(1, D),
        "wproj": np.ascontiguousarray(np.asarray(inputs["L1_attn_w_proj"], f32)),
        "fWT": tr(inputs["L1_ffn_weight"]),
        "fbias": np.asarray(inputs["L1_ffn_bias"], f32).reshape(1, D),
        "ident": np.eye(128, dtype=f32),
        "ones_r": np.ones((1, 128), f32),
        "bd": np.repeat(np.eye(NH, dtype=f32), HDim, axis=0),
        "maskT": np.ascontiguousarray(np.triu(np.ones((GS, GS), f32))),
    }
    nc = build_nc()
    in_maps = []
    for c in range(8):
        m = dict(common)
        m["x"] = np.ascontiguousarray(x[c * T:(c + 1) * T])
        in_maps.append(m)
    res = run_bass_kernel_spmd(nc, in_maps, core_ids=list(range(8)))
    out = np.concatenate([res.results[c]["out"] for c in range(8)], axis=0)
    return out.reshape(2, 8192, 1024)


# version 10
# speedup vs baseline: 1.2502x; 1.2502x over previous
import contextlib
from collections import deque

import numpy as np
import concourse.bass as bass
import concourse.mybir as mybir
from concourse.bass_utils import run_bass_kernel_spmd

F32 = mybir.dt.float32
BF16 = mybir.dt.bfloat16
FP8 = mybir.dt.float8e4
U8 = mybir.dt.uint8
AF = mybir.ActivationFunctionType
ALU = mybir.AluOpType
AX = mybir.AxisListType

T, D, NH, HDim, NE, HID, GS = 2048, 1024, 16, 64, 16, 512, 256
NG = T // GS
ENGINES = ("sync", "vector", "scalar", "tensor", "gpsimd")


class B:
    def __init__(self):
        self.ops = []
        self.eng_ops = {e: [] for e in ENGINES}
        self.last_w = {}
        self.readers = {}
        self.bar_groups = []

    def add(self, engine, emit, reads=(), writes=(), dma=False):
        idx = len(self.ops)
        deps = set()
        for k in reads:
            if k in self.last_w:
                deps.add(self.last_w[k])
        for k in writes:
            if k in self.last_w:
                deps.add(self.last_w[k])
            for r in self.readers.get(k, ()):
                deps.add(r)
        self.ops.append({"eng": engine, "emit": emit, "deps": deps,
                         "dma": dma, "idx": idx, "bar": False})
        self.eng_ops[engine].append(idx)
        for k in writes:
            self.last_w[k] = idx
            self.readers[k] = []
        for k in reads:
            self.readers.setdefault(k, []).append(idx)
        return idx

    def dma(self, engine, dst, src, reads=(), writes=(), accum=False, slow=False):
        kw = {}
        if accum:
            kw["accum_op"] = ALU.add
        if slow:
            kw["allow_slow_non_contiguous"] = True
        emit = lambda e, _d=dst, _s=src, _k=kw: e.dma_start(out=_d, in_=_s, **_k)
        return self.add(engine, emit, reads=reads, writes=writes, dma=True)

    def barrier(self):
        pre = set()
        for e in ENGINES:
            if self.eng_ops[e]:
                pre.add(self.eng_ops[e][-1])
        group = []
        for e in ENGINES:
            i = len(self.ops)
            self.ops.append({"eng": e, "emit": None, "deps": set(pre),
                             "dma": False, "idx": i, "bar": True})
            self.eng_ops[e].append(i)
            group.append(i)
        self.bar_groups.append(group)
        self.last_w = {}
        self.readers = {}

    def finalize(self, nc, block):
        ops = self.ops
        n = len(ops)
        for o in ops:
            if o["bar"]:
                continue
            for g in self.bar_groups:
                if g[-1] < o["idx"]:
                    o["deps"].update(g)
        eng_pos = {}
        for e in ENGINES:
            for pos, i in enumerate(self.eng_ops[e]):
                eng_pos[i] = pos
        for o in ops:
            byeng = {}
            sdeps = []
            for d in o["deps"]:
                de = ops[d]["eng"]
                dk = ops[d]["dma"]
                if de == o["eng"] and (not dk or o["dma"]):
                    if not dk and not o["dma"] and not o["bar"] and not ops[d]["bar"]:
                        sdeps.append(d)
                    continue
                key = (de, dk)
                if key not in byeng or eng_pos[d] > eng_pos[byeng[key]]:
                    byeng[key] = d
            o["x"] = byeng
            o["sdeps"] = sdeps
        adj = [[] for _ in range(n)]
        indeg = [0] * n
        for e in ENGINES:
            seq = self.eng_ops[e]
            for a_, b_ in zip(seq, seq[1:]):
                adj[a_].append(b_)
                indeg[b_] += 1
        for o in ops:
            for d in o["x"].values():
                adj[d].append(o["idx"])
                indeg[o["idx"]] += 1
        dq = deque(i for i in range(n) if indeg[i] == 0)
        seen = 0
        while dq:
            u = dq.popleft()
            seen += 1
            for v in adj[u]:
                indeg[v] -= 1
                if indeg[v] == 0:
                    dq.append(v)
        if seen != n:
            raise RuntimeError(f"dependency cycle: {n - seen}/{n} ops stuck")
        marked = set()
        for o in ops:
            marked.update(o["x"].values())
        for e in ENGINES:
            for i in self.eng_ops[e]:
                if ops[i]["bar"] or ops[i]["dma"]:
                    marked.add(i)
        fenced = set()
        for o in ops:
            for d in o["x"].values():
                if ops[d]["dma"]:
                    fenced.add(d)
            for d in o["deps"]:
                if ops[d]["dma"] and ops[d]["eng"] == o["eng"] and o["dma"]:
                    fenced.add(d)
        inc_tgt = {}
        fence_val = {}
        ci = {e: 0 for e in ENGINES}
        cd = {e: 0 for e in ENGINES}
        for e in ENGINES:
            for i in self.eng_ops[e]:
                if i in marked:
                    if ops[i]["dma"]:
                        cd[e] += 16
                        inc_tgt[i] = ("d", cd[e])
                        if i in fenced:
                            ci[e] += 1
                            fence_val[i] = ci[e]
                    else:
                        ci[e] += 1
                        inc_tgt[i] = ("i", ci[e])
        sem_i = {e: nc.semaphore(f"si{e}").__enter__() for e in ENGINES}
        sem_d = {e: nc.semaphore(f"sd{e}").__enter__() for e in ENGINES}
        self_ = self

        junk_v, junk_s, junk_g = self.junk
        GAP = 3

        def run(eng_obj, ename):
            wi = {e: 0 for e in ENGINES}
            wd = {e: 0 for e in ENGINES}
            emit_pos = {}
            pos = 0

            def dummy():
                nonlocal pos
                if ename == "scalar":
                    eng_obj.memzero(junk_s[:, 0:32])
                elif ename == "vector":
                    eng_obj.memset(junk_v[:, 0:32], 0.0)
                elif ename == "gpsimd":
                    eng_obj.memset(junk_g[:, 0:32], 0.0)
                else:
                    eng_obj.nop()
                pos += 1

            for i in self_.eng_ops[ename]:
                o = ops[i]
                for d in o.get("sdeps", ()):
                    need = GAP - (pos - emit_pos[d])
                    for _ in range(max(0, need)):
                        dummy()
                for key in sorted(o["x"]):
                    d = o["x"][key]
                    de = key[0]
                    if d in fence_val and not (o["dma"] and ename == de):
                        kind, val = "i", fence_val[d]
                    else:
                        kind, val = inc_tgt[d]
                    if kind == "i":
                        if wi[de] < val:
                            eng_obj.wait_ge(sem_i[de], val)
                            wi[de] = val
                    else:
                        if wd[de] < val:
                            eng_obj.wait_ge(sem_d[de], val)
                            wd[de] = val
                if o["bar"]:
                    eng_obj.nop().then_inc(sem_i[ename], 1)
                    emit_pos[i] = pos
                    continue
                ins = o["emit"](eng_obj)
                emit_pos[i] = pos
                pos += 1
                if i in marked:
                    ins.then_inc(sem_d[ename] if o["dma"] else sem_i[ename],
                                 16 if o["dma"] else 1)
                if i in fence_val:
                    eng_obj.wait_ge(sem_d[ename], inc_tgt[i][1])
                    wd[ename] = max(wd[ename], inc_tgt[i][1])
                    eng_obj.nop().then_inc(sem_i[ename], 1)

        block.sync(lambda e: run(e, "sync"))
        block.vector(lambda e: run(e, "vector"))
        block.scalar(lambda e: run(e, "scalar"))
        block.tensor(lambda e: run(e, "tensor"))
        block.gpsimd(lambda e: run(e, "gpsimd"))


def build_nc():
    import os
    UPTO = float(os.environ.get("K_UPTO", "99"))
    nc = bass.Bass("TRN2", target_bir_lowering=False)
    b = B()

    def din(name, shape, dtype=F32):
        return nc.dram_tensor(name, list(shape), dtype, kind="ExternalInput").ap()

    x_h = din("x", (T, D))
    wqT = din("wqT", (D, D)); wkT = din("wkT", (D, D)); wvT = din("wvT", (D, D))
    woutT = din("woutT", (D, D)); bout_h = din("bout", (1, D))
    gwT = din("gate_wT", (D, NE))
    w1Th = din("w1T", (NE, D, HID)); w3Th = din("w3T", (NE, D, HID))
    w2Th = din("w2T", (NE, HID, D))
    sw1T = din("sw1T", (D, HID)); sw3T = din("sw3T", (D, HID)); sw2T = din("sw2T", (HID, D))
    wqkv = din("wqkv", (D, 3 * D)); lnw_h = din("ln_w", (1, D))
    wproj = din("wproj", (D, D))
    fWT = din("fWT", (D, D)); fbias_h = din("fbias", (1, D))
    ident_h = din("ident", (128, 128))
    onesr_h = din("ones_r", (1, 128))
    bd_h = din("bd", (D, NH))
    maskT_h = din("maskT", (GS, GS))
    out_h = nc.dram_tensor("out", [T, D], F32, kind="ExternalOutput").ap()

    NBIG = 2 * D * T + 16
    big_l = nc.dram_tensor("big_l", [NBIG], F32).ap()
    big_g = nc.dram_tensor("big_g", [4, NBIG], F32).ap()
    k_loc = big_l[0:D * T].rearrange("(r c) -> r c", c=T)
    vt_loc = big_l[D * T:2 * D * T].rearrange("(r c) -> r c", c=D)
    kn_loc = big_l[2 * D * T:2 * D * T + 16].rearrange("(o h) -> o h", o=1)
    k_locb = big_l[0:D * T].bitcast(BF16).rearrange("(s r c) -> s r c", s=2, c=T)
    k_gv = lambda s: big_g[s, 0:D * T].rearrange("(r c) -> r c", c=T)
    k_gvb = lambda s: big_g[s, 0:D * T].bitcast(BF16).rearrange("(p r c) -> p r c", p=2, c=T)
    vt_gv = lambda s: big_g[s, D * T:2 * D * T].rearrange("(r c) -> r c", c=D)
    kn_gv = lambda s: big_g[s, 2 * D * T:2 * D * T + 16].rearrange("(o h) -> o h", o=1)
    q_sp = nc.dram_tensor("q_sp", [D * T], F32).ap()
    q_spb = q_sp.bitcast(BF16).rearrange("(s r c) -> s r c", s=2, c=T)
    g_d = nc.dram_tensor("g_d", [T, D], F32).ap()
    s_d = nc.dram_tensor("s_d", [T, D], F32).ap()
    s1_d = nc.dram_tensor("s1_d", [T, D], F32).ap()
    x1_d = nc.dram_tensor("x1_d", [T, D], F32).ap()
    wqkvp = nc.dram_tensor("wqkvp", [2, 24, 128, 512], F32).ap()
    wqkvpb = wqkvp.bitcast(BF16)
    am_l = nc.dram_tensor("am_l", [1, 1], F32).ap()
    am_gd = nc.dram_tensor("am_g", [1, 1], F32).ap()

    GRP = [[0, 1, 2, 3], [4, 5, 6, 7]]
    ALL8 = [list(range(8))]

    ctx = contextlib.ExitStack()
    e_ = ctx.enter_context
    U0 = e_(nc.sbuf_tensor("U0", [128, 16384], F32))
    U1 = e_(nc.sbuf_tensor("U1", [128, 16384], F32))
    U2 = e_(nc.sbuf_tensor("U2", [128, 4096], F32))
    U3 = e_(nc.sbuf_tensor("U3", [128, 512], F32))
    U4 = e_(nc.sbuf_tensor("U4", [128, 512], F32))
    CN = e_(nc.sbuf_tensor("CN", [128, 1024], F32))
    BBB = e_(nc.sbuf_tensor("BBB", [128, 2048], F32))
    IDn = e_(nc.sbuf_tensor("IDn", [128, 128], F32))
    MK = e_(nc.sbuf_tensor("MK", [128, 512], F32))
    XQB = e_(nc.sbuf_tensor("XQB", [128, 12288], BF16))
    F8s = e_(nc.sbuf_tensor("F8s", [128, 1024], FP8))
    M8 = e_(nc.sbuf_tensor("M8", [128, 1024], U8))
    junk_v = e_(nc.sbuf_tensor("junk_v", [128, 32], F32))
    junk_s = e_(nc.sbuf_tensor("junk_s", [128, 32], F32))
    junk_g = e_(nc.sbuf_tensor("junk_g", [128, 32], F32))
    PS = [e_(nc.psum_tensor(f"P{i}", [128, 512], F32)) for i in range(8)]
    PK = [f"P{i}" for i in range(8)]

    def u0c(c, a, z):
        return U0[:, c * T + a: c * T + z]

    U0B = U0.ap().bitcast(BF16)

    def xh(c, a, z):
        return U0B[:, c * T + a: c * T + z]

    def xl(c, a, z):
        return U0B[:, 8 * T + c * T + a: 8 * T + c * T + z]

    def colblocks(dst2d, src2d, nc_):
        m = src2d.shape[1]
        return (dst2d.rearrange("p (n m) -> p n m", n=nc_),
                src2d.rearrange("(n p) m -> p n m", p=128))

    def mm(ps, lhsT, rhs, start, stop, reads, writes):
        b.add("tensor", lambda e, _o=ps, _l=lhsT, _r=rhs, _s=start, _p=stop:
              e.matmul(_o, _l, _r, start=_s, stop=_p), reads=reads, writes=writes)

    def vop(emit, reads, writes):
        b.add("vector", emit, reads=reads, writes=writes)

    def sop(emit, reads, writes):
        b.add("scalar", emit, reads=reads, writes=writes)

    b.dma("sync", IDn[:, :], ident_h, writes=["IDn"])
    b.dma("sync", CN[0:1, 0:128], onesr_h, writes=["ones"])
    b.dma("sync", *colblocks(CN[:, 128:256], bd_h, 8), writes=["bd"])
    b.dma("sync", *colblocks(MK[:, 0:512], maskT_h, 2), writes=["MK"])
    b.dma("sync", CN[:, 256:264].rearrange("p (c o) -> p c o", c=8),
          lnw_h.rearrange("o (c p) -> p c o", p=128), writes=["lnw"], slow=True)
    b.dma("sync", CN[0:1, 264:264 + 128], onesr_h, writes=["_pad"])
    vop(lambda e: e.tensor_mul(CN[:, 272:280], CN[:, 256:264], CN[:, 256:264]),
        ["lnw"], ["lnw2"])
    vop(lambda e: e.memset(CN[:, 448:449], 1.0), [], ["onescol"])
    b.barrier()

    def ln_16tiles(load_tile, dst_chunk, eps, tps, amax_slot=None):
        for tt in range(16):
            j = tt % 2
            xt = U2[:, 2048 + j * 1024: 2048 + (j + 1) * 1024]
            xk = f"lnx{j}"
            stat = lambda k, _j=j: U3[0:128, _j * 16 + k:_j * 16 + k + 1]
            sk = f"lns{j}"
            load_tile(tt, xt, xk)
            sop(lambda e, _x=xt, _a=stat(0): e.activation(
                BBB[:, 0:1024], _x, AF.Square, bias=0.0, scale=1.0, accum_out=_a),
                [xk], ["lnsq", sk + "a"])
            vop(lambda e, _x=xt, _s=stat(1): e.reduce_sum(_s, _x, axis=AX.X),
                [xk], [sk + "b"])
            vop(lambda e, _s=stat(1), _m=stat(2): e.tensor_scalar_mul(_m, _s, 1.0 / D),
                [sk + "b"], [sk + "m"])
            vop(lambda e, _m=stat(2), _t=stat(3): e.tensor_mul(_t, _m, _m),
                [sk + "m"], [sk + "t"])
            vop(lambda e, _a=stat(0), _t=stat(3), _v=stat(4): e.scalar_tensor_tensor(
                _v, _a, 1.0 / D, _t, ALU.mult, ALU.subtract),
                [sk + "a", sk + "t"], [sk + "v"])
            vop(lambda e, _v=stat(4), _e=eps: e.tensor_scalar_add(_v, _v, _e),
                [sk + "v"], [sk + "v"])
            vop(lambda e, _v=stat(4), _r=stat(5): e.reciprocal(_r, _v),
                [sk + "v"], [sk + "r0"])
            sop(lambda e, _r=stat(5): e.activation(_r, _r, AF.Sqrt, bias=0.0,
                                                   scale=1.0),
                [sk + "r0"], [sk + "r"])
            import os as _os2
            _stat = _os2.environ.get("K_STAT", "")
            if _os2.environ.get("K_RAWX"):
                pass
            elif _stat == "m":
                vop(lambda e, _x=xt, _m=stat(2): e.tensor_scalar(
                    _x, _x, _m, None, ALU.subtract),
                    [xk, sk + "m"], [xk])
            elif _stat == "r":
                vop(lambda e, _x=xt, _r=stat(5): e.tensor_scalar(
                    _x, _x, _r, None, ALU.mult),
                    [xk, sk + "r"], [xk])
            elif amax_slot is None:
                vop(lambda e, _x=xt, _m=stat(2), _r=stat(5): e.tensor_scalar(
                    _x, _x, _m, _r, ALU.subtract, ALU.mult),
                    [xk, sk + "m", sk + "r"], [xk])
            if amax_slot is not None:
                vop(lambda e, _x=xt, _m=stat(2), _r=stat(5): e.tensor_scalar(
                    _x, _x, _m, _r, ALU.subtract, ALU.mult),
                    [xk, sk + "m", sk + "r"], [xk])
                vop(lambda e, _x=xt, _t=stat(6): e.tensor_reduce(
                    _t, _x, axis=AX.X, op=ALU.max, apply_absolute_value=True),
                    [xk], [sk + "am"])
                if tt == 0:
                    vop(lambda e, _t=stat(6), _d=amax_slot: e.tensor_copy(_d, _t),
                        [sk + "am"], ["runamax"])
                else:
                    vop(lambda e, _t=stat(6), _d=amax_slot: e.tensor_max(_d, _d, _t),
                        [sk + "am", "runamax"], ["runamax"])
            for c in range(8):
                pt, pk = PS[tps[c % 2]], PK[tps[c % 2]]
                mm(pt[:, 0:128], xt[:, c * 128:(c + 1) * 128], IDn[:, :],
                   True, True, [xk, "IDn"], [pk])
                hi, lo, dkey = dst_chunk(c, tt)
                vop(lambda e, _d=hi, _p=pt: e.tensor_copy(_d, _p[:, 0:128]),
                    [pk], [dkey + "h"])
                vop(lambda e, _d=lo, _h=hi, _p=pt: e.tensor_tensor(
                    _d, _p[:, 0:128], _h, ALU.subtract),
                    [pk, dkey + "h"], [dkey + "l"])

    class _Done(Exception):
        pass

    def _ck(n):
        if UPTO < n:
            _tapflush()
            raise _Done()

    _taps = []

    def tap(ap, reads):
        _taps.append((ap, list(reads)))

    def _tapflush():
        off = 0
        of = out_h.rearrange("r c -> (r c)")
        for ap, reads in _taps:
            p, w = ap.shape[0], ap.shape[1]
            dst = of[off:off + p * w].rearrange("(p m) -> p m", p=p)
            b.dma("gpsimd", dst, ap, reads=reads, writes=["__tap__"])
            off += p * w
        b.barrier()

    try:
        _ck(1) if False else None
        def load_x(tt, xt, xk):
            b.dma("sync", xt, x_h[tt * 128:(tt + 1) * 128, :], reads=[xk], writes=[xk])

        _ck(1)
        ln_16tiles(load_x,
                   lambda c, tt: (xh(c, tt * 128, (tt + 1) * 128),
                                  xl(c, tt * 128, (tt + 1) * 128), f"xn.c{c}"),
                   1e-5, (0, 1))

        U2B = U2.ap().bitcast(BF16)
        BBBB = BBB.ap().bitcast(BF16)
        WF32 = U2[:, 0:1024]
        WHB = U2B[:, 2048:3072]
        WLB = U2B[:, 3072:4096]

        def wsplit(wT_d, dc, wk):
            b.dma("sync", *colblocks(WF32, wT_d[:, dc * 128:(dc + 1) * 128], 8),
                  reads=[wk + "f"], writes=[wk + "f"])
            vop(lambda e: e.tensor_copy(WHB[:, :], WF32), [wk + "f", wk], [wk + "h"])
            vop(lambda e: e.tensor_tensor(WLB[:, :], WF32, WHB[:, :], ALU.subtract),
                [wk + "f", wk + "h", wk], [wk + "l"])

        def mm3(pt, pk, wkk, kc, rhs_h, rhs_l, rkeys, first, last):
            wh = WHB[:, kc * 128:(kc + 1) * 128]
            wl = WLB[:, kc * 128:(kc + 1) * 128]
            mm(pt, wh, rhs_h, first, False, [wkk + "h"] + rkeys, [pk])
            mm(pt, wh, rhs_l, False, False, [wkk + "h"] + rkeys, [pk])
            mm(pt, wl, rhs_h, False, last, [wkk + "l"] + rkeys, [pk])

        def proj_l0(wT_d, post, norm_reduce_key=None, nb=300):
            npsi = [2, 3, 6, 7]
            for dc in range(8):
                wk = "wblk"
                wsplit(wT_d, dc, wk)
                for tk in range(4):
                    pt, pk = PS[4 + (tk % 2)], PK[4 + (tk % 2)]
                    for kc in range(8):
                        mm3(pt[:, :], pk, wk, kc,
                            xh(kc, tk * 512, (tk + 1) * 512),
                            xl(kc, tk * 512, (tk + 1) * 512),
                            [f"xn.c{kc}h", f"xn.c{kc}l"], kc == 0, kc == 7)
                    post(dc, tk, pt, pk, npsi[tk])
                vop(lambda e: e.memset(CN[:, 1023:1024], 0.0),
                    [wk + "h", wk + "l"], [wk])
            if norm_reduce_key is None:
                return
            for tk in range(4):
                vop(lambda e, _tk=tk, _n=npsi[tk], _k=norm_reduce_key, _nb=nb: e.tensor_reduce(
                    CN[0:16, _nb + _tk:_nb + 1 + _tk], PS[_n][0:16, 0:512],
                    axis=AX.X, op=ALU.max), [PK[npsi[tk]]], [f"{norm_reduce_key}{tk}"])
            vop(lambda e, _nb=nb: e.tensor_max(CN[0:16, _nb:_nb + 1], CN[0:16, _nb:_nb + 1],
                                       CN[0:16, _nb + 1:_nb + 2]),
                [f"{norm_reduce_key}0", f"{norm_reduce_key}1"], [f"{norm_reduce_key}0"])
            vop(lambda e, _nb=nb: e.tensor_max(CN[0:16, _nb + 2:_nb + 3], CN[0:16, _nb + 2:_nb + 3],
                                       CN[0:16, _nb + 3:_nb + 4]),
                [f"{norm_reduce_key}2", f"{norm_reduce_key}3"], [f"{norm_reduce_key}2"])
            vop(lambda e, _nb=nb: e.tensor_max(CN[0:16, _nb + 4:_nb + 5], CN[0:16, _nb:_nb + 1],
                                       CN[0:16, _nb + 2:_nb + 3]),
                [f"{norm_reduce_key}0", f"{norm_reduce_key}2"], [f"{norm_reduce_key}M"])

        def sq_norm_mm(src_ap, src_key, tk, dc, npi):
            sq = U4[:, 0:512]
            sop(lambda e, _s=src_ap, _q=sq: e.activation(
                _q, _s, AF.Square, bias=0.0, scale=1.0), [src_key], ["nsq"])
            mm(PS[npi][0:16, 0:512], CN[:, 128 + dc * 16:128 + (dc + 1) * 16], sq,
               dc == 0, dc == 7, ["nsq", "bd"], [PK[npi]])

        U1B = U1.ap().bitcast(BF16)

        QDBG = os.environ.get("K_QDBG", "")

        def q_post(dc, tk, pt, pk, npi):
            dh = U1B[:, dc * T + tk * 512: dc * T + (tk + 1) * 512]
            dl = U1B[:, 8 * T + dc * T + tk * 512: 8 * T + dc * T + (tk + 1) * 512]
            dk = f"q.{dc}.{tk}"
            vop(lambda e, _d=dh, _p=pt: e.tensor_copy(_d, _p[:, :]), [pk], [dk + "h"])
            vop(lambda e, _d=dl, _h=dh, _p=pt: e.tensor_tensor(
                _d, _p[:, :], _h, ALU.subtract), [pk, dk + "h"], [dk + "l"])
            if QDBG != "nonorm":
                sq_norm_mm(dh, dk + "h", tk, dc, npi)
            if QDBG != "nospill":
                b.dma("scalar", q_spb[0, dc * 128:(dc + 1) * 128, tk * 512:(tk + 1) * 512],
                      dh, reads=[dk + "h"], writes=[f"qsp{dc}.{tk}"])
                b.dma("scalar", q_spb[1, dc * 128:(dc + 1) * 128, tk * 512:(tk + 1) * 512],
                      dl, reads=[dk + "l"], writes=[f"qspl{dc}.{tk}"])

        if UPTO < 3:
            import os as _os3
            nch = 8
            if _os3.environ.get("K_STTAP"):
                tap(U3[0:128, 0:32], ["lns0m", "lns1m", "lns0b", "lns1b"])
                nch = 7
            for c in range(nch):
                tap(u0c(c, 0, T), [f"xn.c{c}"])
        def k_post(dc, tk, pt, pk, npi):
            j = tk % 2
            sh = BBBB[:, 2048 + j * 512:2048 + j * 512 + 512]
            sl = BBBB[:, 3072 + j * 512:3072 + j * 512 + 512]
            sk = f"pstg{j}"
            vop(lambda e, _s=sh, _p=pt: e.tensor_copy(_s, _p[:, :]), [pk], [sk + "h"])
            vop(lambda e, _s=sl, _h=sh, _p=pt: e.tensor_tensor(
                _s, _p[:, :], _h, ALU.subtract), [pk, sk + "h"], [sk + "l"])
            b.dma("scalar", k_locb[0, dc * 128:(dc + 1) * 128, tk * 512:(tk + 1) * 512],
                  sh, reads=[sk + "h"], writes=[f"kd{dc}.{tk}"])
            b.dma("scalar", k_locb[1, dc * 128:(dc + 1) * 128, tk * 512:(tk + 1) * 512],
                  sl, reads=[sk + "l"], writes=[f"kdl{dc}.{tk}"])
            sq_norm_mm(sh, sk + "h", tk, dc, npi)

        _ck(2.25)
        proj_l0(wkT, k_post, "kn")
        mm(PS[2][0:1, 0:16], CN[0:16, 304:305], IDn[0:16, 0:16], True, True,
           ["knM", "IDn"], [PK[2]])
        sop(lambda e: e.activation(CN[0:1, 336:352], PS[2][0:1, 0:16], AF.Sqrt,
                                   bias=0.0, scale=1.0), [PK[2]], ["kmr"])
        b.dma("scalar", kn_loc, CN[0:1, 336:352], reads=["kmr"], writes=["knd"])

        VTEST = os.environ.get("K_VTEST", "full")

        def v_post(dc, tk, pt, pk, npi):
            stg = BBB[:, 1024 + (tk % 2) * 512: 1024 + (tk % 2 + 1) * 512]
            sk = f"pstg{tk % 2}"
            vop(lambda e, _s=stg, _p=pt: e.tensor_copy(_s, _p[:, :]),
                [pk, sk + "h", sk + "l"], [sk, sk + "h", sk + "l"])
            for ts in range(4):
                if VTEST in ("full", "nodma"):
                    mm(PS[0][:, 0:128], stg[:, ts * 128:(ts + 1) * 128], IDn[:, :],
                       True, True, [sk, "IDn"], [PK[0]])
                vstg = CN[:, 768:896]
                if VTEST in ("full", "nodma"):
                    vop(lambda e, _p=PS[0], _v=vstg: e.tensor_copy(_v, _p[:, 0:128]),
                        [PK[0]], ["vstg"])
                if VTEST == "full":
                    b.dma("gpsimd",
                          vt_loc[tk * 512 + ts * 128: tk * 512 + (ts + 1) * 128,
                                 dc * 128:(dc + 1) * 128],
                          vstg, reads=["vstg"], writes=[f"vtd{dc}.{tk}.{ts}"])
                elif VTEST == "dmaonly":
                    vop(lambda e, _v=vstg: e.memset(_v, 1.0), [], ["vstg"])
                    b.dma("gpsimd",
                          vt_loc[tk * 512 + ts * 128: tk * 512 + (ts + 1) * 128,
                                 dc * 128:(dc + 1) * 128],
                          vstg, reads=["vstg"], writes=[f"vtd{dc}.{tk}.{ts}"])

        _ck(2.5)
        proj_l0(wvT, v_post)

        _ck(2.75)
        kd_keys = [f"kd{dc}.{tk}" for dc in range(8) for tk in range(4)]
        vt_keys = [f"vtd{dc}.{tk}.{ts}" for dc in range(8) for tk in range(4)
                   for ts in range(4)]
        b.add("gpsimd", lambda e: e.collective_compute(
            "AllGather", ALU.bypass, replica_groups=GRP, ins=[big_l], outs=[big_g]),
            reads=kd_keys + vt_keys + ["knd"], writes=["k_g", "vt_g", "kn_g"])
        _ck(2)
        proj_l0(wqT, q_post, "qn", nb=284)
        mm(PS[2][0:1, 0:16], CN[0:16, 288:289], IDn[0:16, 0:16], True, True,
           ["qnM", "IDn"], [PK[2]])
        sop(lambda e: e.activation(CN[0:1, 320:336], PS[2][0:1, 0:16], AF.Sqrt,
                                   bias=0.0, scale=1.0), [PK[2]], ["qmr"])

        b.barrier()

        _ck(3)
        b.dma("sync", *colblocks(U2[:, 2048:2176], gwT, 8), writes=["gwf"])
        U2B3 = U2.ap().bitcast(BF16)
        GWH = U2B3[:, 4352:4480]
        GWL = U2B3[:, 4480:4608]
        vop(lambda e: e.tensor_copy(GWH[:, :], U2[:, 2048:2176]), ["gwf"], ["gwh"])
        vop(lambda e: e.tensor_tensor(GWL[:, :], U2[:, 2048:2176], GWH[:, :],
                                      ALU.subtract), ["gwf", "gwh"], ["gwl"])
        for tc in range(16):
            pt, pk = PS[6], PK[6]
            for kc in range(8):
                mm(pt[:, 0:16], xh(kc, tc * 128, (tc + 1) * 128),
                   GWH[:, kc * 16:(kc + 1) * 16], kc == 0, False,
                   [f"xn.c{kc}h", "gwh"], [pk])
                mm(pt[:, 0:16], xl(kc, tc * 128, (tc + 1) * 128),
                   GWH[:, kc * 16:(kc + 1) * 16], False, False,
                   [f"xn.c{kc}l", "gwh"], [pk])
                mm(pt[:, 0:16], xh(kc, tc * 128, (tc + 1) * 128),
                   GWL[:, kc * 16:(kc + 1) * 16], False, kc == 7,
                   [f"xn.c{kc}h", "gwl"], [pk])
            lg = U3[:, 0:16]
            mx8 = U3[:, 16:24]
            s1_ = U3[:, 24:25]
            s2_ = U3[:, 25:26]
            wk_ = U3[:, 32:48]
            vop(lambda e, _p=pt, _l=lg: e.tensor_copy(_l, _p[0:128, 0:16]), [pk], ["lg"])
            vop(lambda e, _l=lg, _m=s1_: e.reduce_max(_m, _l, axis=AX.X), ["lg"], ["lgm"])
            vop(lambda e, _m=s1_: e.tensor_scalar_mul(_m, _m, -1.0), ["lgm"], ["lgm"])
            sop(lambda e, _l=lg, _m=s1_, _s=s2_: e.activation(
                _l, _l, AF.Exp, bias=_m, scale=1.0, accum_out=_s),
                ["lg", "lgm"], ["lg", "lgs"])
            vop(lambda e, _s=s2_: e.reciprocal(_s, _s), ["lgs"], ["lgs"])
            vop(lambda e, _l=lg, _s=s2_: e.tensor_scalar_mul(_l, _l, _s), ["lg", "lgs"], ["lg"])
            vop(lambda e, _l=lg, _m=mx8: e.max(out=_m, in_=_l), ["lg"], ["mx8"])
            vop(lambda e, _m=mx8: e.memset(_m[:, 4:8], 0.0), ["mx8"], ["mx8"])
            vop(lambda e, _l=lg, _m=mx8, _w=wk_: e.match_replace(
                out=_w, in_to_replace=_m, in_values=_l, imm_value=0.0),
                ["lg", "mx8"], ["wk"])
            vop(lambda e, _m=mx8, _s=s1_: e.reduce_sum(_s, _m[:, 0:4], axis=AX.X),
                ["mx8"], ["tsum"])
            vop(lambda e, _s=s1_: e.reciprocal(_s, _s), ["tsum"], ["tsum"])
            vop(lambda e, _l=lg, _w=wk_: e.tensor_sub(_w, _l, _w), ["lg", "wk"], ["wk"])
            vop(lambda e, _w=wk_, _s=s1_, _tc=tc: e.tensor_scalar_mul(
                CN[:, 512 + _tc * 16:512 + (_tc + 1) * 16], _w, _s),
                ["wk", "tsum"], [f"cm{tc}"])

        U1B3 = U1.ap().bitcast(BF16)
        ATH = lambda hc: U1B3[:, hc * 2048:(hc + 1) * 2048]
        ATL = lambda hc: U1B3[:, 8192 + hc * 2048:8192 + (hc + 1) * 2048]
        W1F = U1[:, 8192:9216]
        W1H = U1B3[:, 18432:19456]
        W1L = U1B3[:, 19456:20480]
        W3F = U1[:, 10240:11264]
        W3H = U1B3[:, 22528:23552]
        W3L = U1B3[:, 23552:24576]
        W2H = U1B3[:, 24576:28672]
        W2L = U1B3[:, 28672:32768]

        def split_blk(dst_h, dst_l, srcf, kf, kh_, kl_):
            vop(lambda e, _h=dst_h, _s=srcf: e.tensor_copy(_h, _s), [kf, kh_], [kh_])
            vop(lambda e, _l=dst_l, _s=srcf, _h=dst_h: e.tensor_tensor(
                _l, _s, _h, ALU.subtract), [kf, kh_, kl_], [kl_])

        def expert(ei, w1d, w3d, w2d, shared):
            for hc in range(4):
                b.dma("sync", *colblocks(W1F, w1d[:, hc * 128:(hc + 1) * 128], 8),
                      reads=["w1f", "w1h", "w1l"], writes=["w1f"])
                split_blk(W1H, W1L, W1F, "w1f", "w1h", "w1l")
                b.dma("sync", *colblocks(W3F, w3d[:, hc * 128:(hc + 1) * 128], 8),
                      reads=["w3f", "w3h", "w3l"], writes=["w3f"])
                split_blk(W3H, W3L, W3F, "w3f", "w3h", "w3l")
                for kc in range(8):
                    for wh_, wl_, kh_, kl_, pb in ((W1H, W1L, "w1h", "w1l", 0),
                                                   (W3H, W3L, "w3h", "w3l", 4)):
                        for tk in range(4):
                            mm(PS[pb + tk][:, :], wh_[:, kc * 128:(kc + 1) * 128],
                               xh(kc, tk * 512, (tk + 1) * 512), kc == 0, False,
                               [kh_, f"xn.c{kc}h"], [PK[pb + tk]])
                        for tk in range(4):
                            mm(PS[pb + tk][:, :], wh_[:, kc * 128:(kc + 1) * 128],
                               xl(kc, tk * 512, (tk + 1) * 512), False, False,
                               [kh_, f"xn.c{kc}l"], [PK[pb + tk]])
                        for tk in range(4):
                            mm(PS[pb + tk][:, :], wl_[:, kc * 128:(kc + 1) * 128],
                               xh(kc, tk * 512, (tk + 1) * 512), False, kc == 7,
                               [kl_, f"xn.c{kc}h"], [PK[pb + tk]])
                for tk in range(4):
                    sl = U2[:, 1024 + (tk % 2) * 512:1024 + (tk % 2 + 1) * 512]
                    slk = f"silu{tk % 2}"
                    sop(lambda e, _s=sl, _tk=tk: e.activation(
                        _s, PS[_tk][:, :], AF.Silu, bias=0.0, scale=1.0),
                        [PK[tk]], [slk])
                    pr = U2[:, 3072 + (tk % 2) * 512:3072 + (tk % 2 + 1) * 512]
                    prk = f"prod{tk % 2}"
                    vop(lambda e, _s=sl, _tk=tk, _p=pr: e.tensor_mul(
                        _p, _s, PS[4 + _tk][:, :]), [slk, PK[4 + tk]], [prk])
                    vop(lambda e, _p=pr, _hc=hc, _tk=tk: e.tensor_copy(
                        ATH(_hc)[:, _tk * 512:(_tk + 1) * 512], _p),
                        [prk], [f"act{hc}.{tk}h"])
                    vop(lambda e, _p=pr, _hc=hc, _tk=tk: e.tensor_tensor(
                        ATL(_hc)[:, _tk * 512:(_tk + 1) * 512], _p,
                        ATH(_hc)[:, _tk * 512:(_tk + 1) * 512], ALU.subtract),
                        [prk, f"act{hc}.{tk}h"], [f"act{hc}.{tk}l"])
            for hc in range(4):
                b.dma("sync", W1F, w2d[hc * 128:(hc + 1) * 128, :],
                      reads=["w1f", "w1h", "w1l", f"w2h{hc}", f"w2l{hc}"],
                      writes=["w1f"])
                split_blk(W2H[:, hc * 1024:(hc + 1) * 1024],
                          W2L[:, hc * 1024:(hc + 1) * 1024], W1F,
                          "w1f", f"w2h{hc}", f"w2l{hc}")
            for tc in range(16):
                for half in range(2):
                    sl2 = (tc * 2 + half) % 2
                    pt, pk = PS[sl2], PK[sl2]
                    for hc in range(4):
                        ah = ATH(hc)[:, tc * 128:(tc + 1) * 128]
                        al = ATL(hc)[:, tc * 128:(tc + 1) * 128]
                        w2hs = W2H[:, hc * 1024 + half * 512:hc * 1024 + (half + 1) * 512]
                        w2ls = W2L[:, hc * 1024 + half * 512:hc * 1024 + (half + 1) * 512]
                        ka = [f"act{hc}.{tc // 4}h", f"act{hc}.{tc // 4}l",
                              f"w2h{hc}", f"w2l{hc}"]
                        mm(pt[:, :], ah, w2hs, hc == 0, False, ka, [pk])
                        mm(pt[:, :], al, w2hs, False, False, ka, [pk])
                        mm(pt[:, :], ah, w2ls, False, hc == 3, ka, [pk])
                    stg = U2[:, sl2 * 512:(sl2 + 1) * 512]
                    sk = f"w2s{sl2}"
                    if shared:
                        vop(lambda e, _s=stg, _p=pt: e.tensor_copy(_s, _p[:, :]),
                            [pk], [sk])
                    else:
                        vop(lambda e, _s=stg, _p=pt, _tc=tc, _e=ei: e.tensor_scalar_mul(
                            _s, _p[:, :], CN[:, 512 + _tc * 16 + _e:513 + _tc * 16 + _e]),
                            [pk, f"cm{tc}"], [sk])
                    b.dma("gpsimd",
                          g_d[tc * 128:(tc + 1) * 128, half * 512:(half + 1) * 512],
                          stg, reads=[sk], writes=[f"gacc{tc}.{half}"],
                          accum=(ei > 0))

        for ei in range(NE):
            expert(ei, w1Th[ei], w3Th[ei], w2Th[ei], False)
        expert(NE, sw1T, sw3T, sw2T, True)
        b.barrier()

        if UPTO < 5:
            _sel = os.environ.get("K_TAPSEL", "q")
            if _sel == "q":
                for dc in range(8):
                    tap(q_sp[dc * 128:(dc + 1) * 128, :], ["qsp0.0"])
            elif _sel == "g":
                for tc in range(16):
                    tap(g_d[tc * 128:(tc + 1) * 128, :],
                        [f"gacc{tc}.0", f"gacc{tc}.1"])
            elif _sel == "cm":
                tap(CN[0:128, 512:768], [f"cm{t}" for t in range(16)])
            elif _sel == "kv":
                for dc in range(8):
                    tap(k_loc[dc * 128:(dc + 1) * 128, :], ["kd7.3"])
            elif _sel == "vt":
                for tc in range(16):
                    tap(vt_loc[tc * 128:(tc + 1) * 128, :], ["vtd7.3.3"])
        _ck(5)
        for s4 in range(4):
            b.dma("sync", CN[0:1, 352 + s4 * 16:368 + s4 * 16], kn_gv(s4),
                  reads=["kn_g"], writes=["kng"])
        vop(lambda e: e.tensor_max(CN[0:1, 352:368], CN[0:1, 352:368], CN[0:1, 368:384]),
            ["kng"], ["kng"])
        vop(lambda e: e.tensor_max(CN[0:1, 384:400], CN[0:1, 384:400], CN[0:1, 400:416]),
            ["kng"], ["kngb"])
        vop(lambda e: e.tensor_max(CN[0:1, 352:368], CN[0:1, 352:368], CN[0:1, 384:400]),
            ["kng", "kngb"], ["kmx"])
        vop(lambda e: e.tensor_mul(CN[0:1, 416:432], CN[0:1, 320:336], CN[0:1, 352:368]),
            ["kmx"], ["cr"])
        vop(lambda e: e.tensor_scalar_mul(CN[0:1, 416:432], CN[0:1, 416:432], -0.125),
            ["cr"], ["cr"])
        mm(PS[4][:, 0:16], CN[0:1, 0:128], CN[0:1, 416:432], True, True,
           ["cr"], [PK[4]])
        vop(lambda e: e.tensor_copy(CN[:, 432:448], PS[4][:, 0:16]), [PK[4]], ["negc"])

        U1B5 = U1.ap().bitcast(BF16)
        QHh = U1B5[0:64, 0:2048]
        QHl = U1B5[0:64, 2048:4096]
        KTh = lambda j: U1B5[0:64, 4096 + j * 512:4096 + (j + 1) * 512]
        KTlo = lambda j: U1B5[0:64, 5120 + j * 512:5120 + (j + 1) * 512]
        FR = lambda j: U1[:, 3072 + j * 512:3072 + (j + 1) * 512]
        PTl = lambda i: U1[:, 4096 + i * 512:4096 + (i + 1) * 512]
        RR = U1[0:1, 6144:6656]
        ANS = U1[:, 6656:7168]
        for j in range(2):
            for kc4 in range(4):
                vop(lambda e, _j=j, _k=kc4: e.memset(
                    FR(_j)[:, _k * 128:_k * 128 + 1], 1.0), [], [f"fri{j}.{kc4}"])
                vop(lambda e, _j=j, _k=kc4: e.memset(
                    FR(_j)[:, _k * 128 + 1:_k * 128 + 64], 0.0), [], [f"frz{j}.{kc4}"])

        for h in range(NH):
            b.dma("sync", QHh, q_spb[0, h * 64:(h + 1) * 64, :],
                  reads=["qh"], writes=["qh"])
            b.dma("sync", QHl, q_spb[1, h * 64:(h + 1) * 64, :],
                  reads=["qh"], writes=["qh"])
            for kvb in range(16):
                j = kvb % 2
                src = (kvb * 512) // 2048
                col = (kvb * 512) % 2048
                b.dma("sync", KTh(j), k_gvb(src)[0, h * 64:(h + 1) * 64, col:col + 512],
                      reads=[f"kt{j}", "k_g"], writes=[f"kt{j}"])
                b.dma("sync", KTlo(j), k_gvb(src)[1, h * 64:(h + 1) * 64, col:col + 512],
                      reads=[f"kt{j}", "k_g"], writes=[f"kt{j}"])
                b.dma("sync",
                      FR(j).rearrange("p (kc c) -> p kc c", kc=4)[:, :, 64:128],
                      vt_gv(src)[col:col + 512, h * 64:(h + 1) * 64].rearrange(
                          "(kc p) c -> p kc c", p=128),
                      reads=[f"fr{j}", "vt_g"], writes=[f"fr{j}"])
                for qt in range(4):
                    for kc4 in range(4):
                        pss, psk = PS[6 + (kc4 % 2)], PK[6 + (kc4 % 2)]
                        mm(pss[:, :], KTh(j)[:, kc4 * 128:(kc4 + 1) * 128],
                           QHh[:, qt * 512:(qt + 1) * 512], True, False,
                           [f"kt{j}", "qh"], [psk])
                        mm(pss[:, :], KTh(j)[:, kc4 * 128:(kc4 + 1) * 128],
                           QHl[:, qt * 512:(qt + 1) * 512], False, False,
                           [f"kt{j}", "qh"], [psk])
                        mm(pss[:, :], KTlo(j)[:, kc4 * 128:(kc4 + 1) * 128],
                           QHh[:, qt * 512:(qt + 1) * 512], False, True,
                           [f"kt{j}", "qh"], [psk])
                        sop(lambda e, _p=pss, _h=h, _o=PTl(kc4): e.activation(
                            _o, _p[:, :], AF.Exp, bias=CN[:, 432 + _h:433 + _h],
                            scale=0.125), [psk, "negc"], [f"pt{kc4}"])
                    for kc4 in range(4):
                        mm(PS[qt][:, :], FR(j)[:, kc4 * 128:(kc4 + 1) * 128],
                           PTl(kc4), (kvb == 0 and kc4 == 0), (kvb == 15 and kc4 == 3),
                           [f"fr{j}", f"fri{j}.{kc4}", f"frz{j}.{kc4}", f"pt{kc4}"],
                           [PK[qt]])
            for qt in range(4):
                vop(lambda e, _qt=qt: e.tensor_copy(RR, PS[_qt][0:1, :]),
                    [PK[qt]], ["rr"])
                vop(lambda e: e.reciprocal(RR, RR), ["rr"], ["rr"])
                mm(PS[4][:, :], CN[0:1, 0:128], RR, True, True, ["rr"], [PK[4]])
                RB = U1[:, 7168:7680]
                vop(lambda e: e.tensor_copy(RB, PS[4][:, :]), [PK[4]], ["rb"])
                if h % 2 == 1:
                    vop(lambda e, _h=h, _qt=qt: e.tensor_mul(
                        u0c(_h // 2, _qt * 512, (_qt + 1) * 512)[64:128, :],
                        PS[_qt][64:128, :], RB[64:128, :]),
                        [PK[qt], "rb"], [f"a.{h // 2}.hi"])
                else:
                    vop(lambda e, _qt=qt: e.tensor_mul(
                        ANS[64:128, :], PS[_qt][64:128, :], RB[64:128, :]),
                        [PK[qt], "rb"], ["ans"])
                    b.dma("scalar",
                          u0c(h // 2, qt * 512, (qt + 1) * 512)[0:64, :],
                          ANS[64:128, :], reads=["ans"], writes=[f"a.{h // 2}.lo"])
        b.barrier()

        if UPTO < 6:
            for c in range(8):
                tap(u0c(c, 0, T), [f"a.c{c}" if False else f"a.{c}.lo", f"a.{c}.hi"])
        _ck(6)
        b.dma("sync", CN[0:1, 264:264 + D] if False else BBB[0:1, 0:D], bout_h,
              writes=["boutr"])
        for half in range(2):
            mm(PS[6][:, 0:512], CN[0:1, 0:128], BBB[0:1, half * 512:(half + 1) * 512],
               True, True, ["boutr"], [PK[6]])
            vop(lambda e, _h=half: e.tensor_copy(
                BBB[:, 1024 + _h * 512:1024 + (_h + 1) * 512], PS[6][:, 0:512]),
                [PK[6], "boutr"], [f"boutb{half}"])
        for half in range(2):
            wt = U1[:, 8192 + half * 4096: 8192 + (half + 1) * 4096]
            wk = f"wo{half}"
            b.dma("sync", *colblocks(wt, woutT[:, half * 512:(half + 1) * 512], 8),
                  writes=[wk])
            for tc in range(16):
                pt, pk = PS[tc % 2], PK[tc % 2]
                for kc in range(8):
                    mm(pt[:, :], u0c(kc, tc * 128, (tc + 1) * 128),
                       wt[:, kc * 512:(kc + 1) * 512], kc == 0, kc == 7, [wk], [pk])
                stg = U3[:, 0:512]
                vop(lambda e, _p=pt, _s=stg, _h=half: e.tensor_add(
                    _s, _p[:, :], BBB[:, 1024 + _h * 512:1024 + (_h + 1) * 512]),
                    [pk, f"boutb{half}"], ["sstg"])
                b.dma("gpsimd", s_d[tc * 128:(tc + 1) * 128, half * 512:(half + 1) * 512],
                      stg, reads=["sstg"], writes=[f"sd{tc}.{half}"])
        b.barrier()

        if UPTO < 7:
            for tc in range(16):
                tap(s_d[tc * 128:(tc + 1) * 128, :], [f"sd{tc}.0", f"sd{tc}.1"])
        _ck(7)
        for tc in range(16):
            xt = U2[:, 0:1024]
            st_ = U2[:, 1024:2048]
            gt = U2[:, 2048:3072]
            b.dma("sync", xt, x_h[tc * 128:(tc + 1) * 128, :], reads=["p7x"], writes=["p7x"])
            b.dma("sync", st_, s_d[tc * 128:(tc + 1) * 128, :], reads=["p7s"], writes=["p7s"])
            b.dma("sync", gt, g_d[tc * 128:(tc + 1) * 128, :], reads=["p7g"], writes=["p7g"])
            sop(lambda e, _g=gt: e.activation(_g, _g, AF.Silu, bias=0.0, scale=1.0),
                ["p7g"], ["p7g"])
            vop(lambda e, _s=st_, _g=gt: e.tensor_mul(_s, _s, _g), ["p7s", "p7g"], ["p7s"])
            vop(lambda e, _x=xt, _s=st_, _tc=tc: e.tensor_add(
                U1[:, _tc * 1024:(_tc + 1) * 1024], _x, _s),
                ["p7x", "p7s"], [f"x1.{tc}"])
        b.barrier()

        if UPTO < 8:
            for tc in range(16):
                tap(U1[:, tc * 1024:(tc + 1) * 1024], [f"x1.{tc}"])
        _ck(8)
        def load_x1(tt, xt, xk):
            vop(lambda e, _x=xt, _tt=tt: e.tensor_copy(_x, U1[:, _tt * 1024:(_tt + 1) * 1024]),
                [xk, f"x1.{tt}"], [xk])

        ln_16tiles(load_x1,
                   lambda c, tt: (xh(c, tt * 128, (tt + 1) * 128),
                                  xl(c, tt * 128, (tt + 1) * 128), f"xn1.c{c}"),
                   1e-5, (0, 1), amax_slot=CN[:, 960:961])
        for tc in range(16):
            b.dma("scalar", x1_d[tc * 128:(tc + 1) * 128, :],
                  U1[:, tc * 1024:(tc + 1) * 1024], reads=[f"x1.{tc}"],
                  writes=[f"x1d{tc}"])
        _ck(7.3)
        mm(PS[5][0:1, 0:128], CN[:, 960:961], IDn[:, :], True, True,
           ["runamax", "IDn"], [PK[5]])
        vop(lambda e: e.tensor_reduce(CN[0:1, 961:962], PS[5][0:1, 0:128],
                                      axis=AX.X, op=ALU.max), [PK[5]], ["am1"])
        b.dma("scalar", am_l, CN[0:1, 961:962], reads=["am1"], writes=["amld"])
        b.add("gpsimd", lambda e: e.collective_compute(
            "AllReduce", ALU.max, replica_groups=ALL8, ins=[am_l], outs=[am_gd]),
            reads=["amld"], writes=["amg"])
        _ck(7.6)
        WQF = U2[:, 0:1024]
        U2B8 = U2.ap().bitcast(BF16)
        WQH8 = U2B8[:, 2048:3072]
        WQL8 = U2B8[:, 3072:4096]
        for dc in range(24):
            b.dma("sync", *colblocks(WQF, wqkv[:, dc * 128:(dc + 1) * 128], 8),
                  reads=["wqf", "wqh8", "wql8"], writes=["wqf"])
            vop(lambda e: e.tensor_copy(WQH8[:, :], WQF), ["wqf", "wqh8"], ["wqh8"])
            vop(lambda e: e.tensor_tensor(WQL8[:, :], WQF, WQH8[:, :], ALU.subtract),
                ["wqf", "wqh8", "wql8"], ["wql8"])
            _wq = os.environ.get("K_WQDBG", "")
            for s, wv, kk in ((0, WQH8, "wqh8"), (1, WQL8, "wql8")):
                if _wq == "nodma":
                    vop(lambda e: e.memset(CN[:, 1022:1023], 0.0),
                        [kk], [f"wqkvp{s}.{dc}"])
                elif _wq == "f32dst":
                    b.dma("gpsimd", wqkvp[s, dc], U2[:, 0:512],
                          reads=[kk, "wqf"], writes=[f"wqkvp{s}.{dc}"])
                else:
                    b.dma("scalar", wqkvpb[s, dc], wv,
                          reads=[kk], writes=[f"wqkvp{s}.{dc}"])
        b.barrier()

        if UPTO < 9:
            for c in range(8):
                tap(u0c(c, 0, T), [f"xn1.c{c}"])
        _ck(9)
        QKVg = lambda dc: U1[:, 2048 + dc * 256:2048 + (dc + 1) * 256]
        FR1 = lambda h, kc2: U1[:, 8192 + (h * 2 + kc2) * 128:8192 + (h * 2 + kc2 + 1) * 128]
        QHs = U1[0:64, 12288:12544]
        KHs = U1[0:64, 12544:12800]
        PPl = lambda kc2: U1[:, 12800 + kc2 * 256:12800 + (kc2 + 1) * 256]
        RR1 = U1[0:1, 13312:13568]
        ANS1 = U1[:, 13568:13824]
        for h in range(NH):
            for kc2 in range(2):
                vop(lambda e, _h=h, _k=kc2: e.memset(FR1(_h, _k)[:, 0:1], 1.0),
                    [], [f"f1i{h}.{kc2}"])
                vop(lambda e, _h=h, _k=kc2: e.memset(FR1(_h, _k)[:, 1:64], 0.0),
                    [], [f"f1z{h}.{kc2}"])

        for g in range(NG):
            g0 = g * GS
            U1B9 = U1.ap().bitcast(BF16)
            WP9H = U1B9[:, 0:1024]
            WP9L = U1B9[:, 1024:2048]
            for dc in range(24):
                b.dma("sync", WP9H, wqkvpb[0, dc],
                      reads=["wqk0", "wqk1", f"wqkvp0.{dc}"], writes=["wqk0"])
                b.dma("sync", WP9L, wqkvpb[1, dc],
                      reads=["wqk0", "wqk1", f"wqkvp1.{dc}"], writes=["wqk1"])
                pt, pk = PS[4 + (dc % 2)], PK[4 + (dc % 2)]
                for kc in range(8):
                    mm(pt[:, 0:256], WP9H[:, kc * 128:(kc + 1) * 128],
                       xh(kc, g0, g0 + 256), kc == 0, False,
                       ["wqk0", f"xn1.c{kc}h"], [pk])
                    mm(pt[:, 0:256], WP9H[:, kc * 128:(kc + 1) * 128],
                       xl(kc, g0, g0 + 256), False, False,
                       ["wqk0", f"xn1.c{kc}l"], [pk])
                    mm(pt[:, 0:256], WP9L[:, kc * 128:(kc + 1) * 128],
                       xh(kc, g0, g0 + 256), False, kc == 7,
                       ["wqk1", f"xn1.c{kc}h"], [pk])
                vop(lambda e, _p=pt, _dc=dc: e.tensor_copy(QKVg(_dc), _p[:, 0:256]),
                    [pk], [f"kv{dc}"])
            for which, base in (("q", 0), ("k", 8)):
                sq = U3[:, 0:256]
                for kc in range(8):
                    vop(lambda e, _b=base, _kc=kc, _s=sq: e.tensor_mul(
                        _s, QKVg(_b + _kc), QKVg(_b + _kc)), [f"kv{base + kc}"], ["sqg"])
                    mm(PS[6][0:1, 0:256], CN[:, 448:449], sq, kc == 0, kc == 7,
                       ["sqg", "onescol"], [PK[6]])
                fr_ = U4[0:1, 0:256]
                vop(lambda e, _f=fr_: e.tensor_scalar(
                    _f, PS[6][0:1, 0:256], 1.0 / D, 1e-6, ALU.mult, ALU.add),
                    [PK[6]], ["f1pre"])
                vop(lambda e, _f=fr_: e.reciprocal(_f, _f), ["f1pre"], ["f1r0"])
                sop(lambda e, _f=fr_: e.activation(_f, _f, AF.Sqrt, bias=0.0,
                                                   scale=1.0), ["f1r0"], ["f1row"])
                mm(PS[6][:, 0:256], CN[0:1, 0:128], fr_, True, True,
                   ["f1row"], [PK[6]])
                for kc in range(8):
                    if which == "q":
                        vop(lambda e, _kc=kc: e.tensor_mul(
                            QKVg(_kc), QKVg(_kc), PS[6][:, 0:256]),
                            [f"kv{kc}", PK[6]], [f"kv{kc}"])
                    else:
                        vop(lambda e, _kc=kc: e.scalar_tensor_tensor(
                            QKVg(8 + _kc), QKVg(8 + _kc),
                            CN[:, 272 + _kc:273 + _kc], PS[6][:, 0:256],
                            ALU.mult, ALU.mult),
                            [f"kv{8 + kc}", "lnw2", PK[6]], [f"kv{8 + kc}"])
            for which, base, col in (("q", 0, 464), ("k", 8, 480)):
                sq = U3[:, 0:256]
                for kc in range(8):
                    vop(lambda e, _b=base, _kc=kc, _s=sq: e.tensor_mul(
                        _s, QKVg(_b + _kc), QKVg(_b + _kc)), [f"kv{base + kc}"], ["sqg"])
                    mm(PS[6][0:16, 0:256], CN[:, 128 + kc * 16:128 + (kc + 1) * 16], sq,
                       kc == 0, kc == 7, ["sqg", "bd"], [PK[6]])
                vop(lambda e, _c=col: e.tensor_reduce(
                    CN[0:16, _c:_c + 1], PS[6][0:16, 0:256], axis=AX.X, op=ALU.max),
                    [PK[6]], [f"n1{which}"])
            vop(lambda e: e.tensor_mul(CN[0:16, 464:465], CN[0:16, 464:465],
                                       CN[0:16, 480:481]), ["n1q", "n1k"], ["n1c"])
            sop(lambda e: e.activation(CN[0:16, 465:466], CN[0:16, 464:465],
                                       AF.Sqrt, bias=0.0, scale=1.0), ["n1c"], ["n1s"])
            vop(lambda e: e.tensor_scalar_mul(CN[0:16, 465:466], CN[0:16, 465:466],
                                              -0.125), ["n1s"], ["n1s"])
            mm(PS[6][0:1, 0:16], CN[0:16, 465:466], IDn[0:16, 0:16], True, True,
               ["n1s", "IDn"], [PK[6]])
            vop(lambda e: e.tensor_copy(U3[0:1, 496:512], PS[6][0:1, 0:16]),
                [PK[6]], ["n1r"])
            mm(PS[6][:, 0:16], CN[0:1, 0:128], U3[0:1, 496:512], True, True,
               ["n1r"], [PK[6]])
            vop(lambda e: e.tensor_copy(CN[:, 496:512], PS[6][:, 0:16]),
                [PK[6]], ["negc1"])
            for dc in range(8):
                for ts in range(2):
                    mm(PS[6][:, 0:128], QKVg(16 + dc)[:, ts * 128:(ts + 1) * 128],
                       IDn[:, :], True, True, [f"kv{16 + dc}", "IDn"], [PK[6]])
                    h0, h1 = 2 * dc, 2 * dc + 1
                    vop(lambda e, _h=h0, _t=ts: e.tensor_copy(
                        FR1(_h, _t)[:, 64:128], PS[6][:, 0:64]),
                        [PK[6]], [f"f1d{h0}.{ts}"])
                    vop(lambda e, _h=h1, _t=ts: e.tensor_copy(
                        FR1(_h, _t)[:, 64:128], PS[6][:, 64:128]),
                        [PK[6]], [f"f1d{h1}.{ts}"])
            for h in range(NH):
                ch, ro = h // 2, (h % 2) * 64
                b.dma("sync", QHs, QKVg(ch)[ro:ro + 64, :],
                      reads=["qhs", f"kv{ch}"], writes=["qhs"])
                b.dma("sync", KHs, QKVg(8 + ch)[ro:ro + 64, :],
                      reads=["khs", f"kv{8 + ch}"], writes=["khs"])
                for kc2 in range(2):
                    mm(PS[6][:, 0:256], KHs[:, kc2 * 128:(kc2 + 1) * 128], QHs,
                       True, True, ["qhs", "khs"], [PK[6]])
                    sop(lambda e, _h=h, _k=kc2: e.activation(
                        PPl(_k), PS[6][:, 0:256], AF.Exp,
                        bias=CN[:, 496 + _h:497 + _h], scale=0.125),
                        [PK[6], "negc1"], [f"pp{kc2}"])
                    vop(lambda e, _k=kc2: e.tensor_mul(
                        PPl(_k), PPl(_k), MK[:, _k * 256:(_k + 1) * 256]),
                        [f"pp{kc2}", "MK"], [f"pp{kc2}"])
                    mm(PS[7][:, 0:256], FR1(h, kc2), PPl(kc2), kc2 == 0, kc2 == 1,
                       [f"pp{kc2}", f"f1d{h}.{kc2}", f"f1i{h}.{kc2}", f"f1z{h}.{kc2}"],
                       [PK[7]])
                vop(lambda e: e.tensor_copy(RR1, PS[7][0:1, 0:256]), [PK[7]], ["rr1"])
                vop(lambda e: e.reciprocal(RR1, RR1), ["rr1"], ["rr1"])
                mm(PS[5][:, 0:256], CN[0:1, 0:128], RR1, True, True, ["rr1"], [PK[5]])
                RB1 = U1[:, 13824:14080]
                vop(lambda e: e.tensor_copy(RB1, PS[5][:, 0:256]), [PK[5]], ["rb1"])
                if h % 2 == 1:
                    vop(lambda e, _ch=ch: e.tensor_mul(
                        QKVg(16 + _ch)[64:128, :], PS[7][64:128, 0:256],
                        RB1[64:128, :]),
                        [PK[7], "rb1", f"kv{16 + ch}"], [f"a1.{ch}.hi", f"kv{16 + ch}"])
                else:
                    vop(lambda e: e.tensor_mul(ANS1[64:128, :], PS[7][64:128, 0:256],
                                               RB1[64:128, :]),
                        [PK[7], "rb1"], ["ans1"])
                    b.dma("scalar", QKVg(16 + ch)[0:64, :], ANS1[64:128, :],
                          reads=["ans1", f"kv{16 + ch}"],
                          writes=[f"a1.{ch}.lo", f"kv{16 + ch}"])
            for q4 in range(4):
                wt = U1[:, 0:2048]
                b.dma("sync", *colblocks(wt, wproj[:, q4 * 256:(q4 + 1) * 256], 8),
                      reads=["wqk0", "wqk1"], writes=["wqk0", "wqk1"])
                for tc2 in range(2):
                    pt, pk = PS[tc2 % 2], PK[tc2 % 2]
                    for kc in range(8):
                        mm(pt[:, 0:256], QKVg(16 + kc)[:, tc2 * 128:(tc2 + 1) * 128],
                           wt[:, kc * 256:(kc + 1) * 256], kc == 0, kc == 7,
                           [f"a1.{kc}.lo", f"a1.{kc}.hi", "wqk0", "wqk1"], [pk])
                    stg = U3[:, 0:256]
                    vop(lambda e, _p=pt, _s=stg: e.tensor_copy(_s, _p[:, 0:256]),
                        [pk], ["s1s"])
                    b.dma("scalar",
                          s1_d[g0 + tc2 * 128:g0 + (tc2 + 1) * 128,
                               q4 * 256:(q4 + 1) * 256],
                          stg, reads=["s1s"], writes=[f"s1d{g}.{tc2}.{q4}"])
        b.barrier()

        if UPTO < 10:
            _sel = os.environ.get("K_TAPSEL", "s1")
            if _sel == "s1":
                for tc in range(16):
                    tap(s1_d[tc * 128:(tc + 1) * 128, :], ["s1d0.0.0"])
            elif _sel == "a1g":
                for dc in range(8):
                    tap(QKVg(16 + dc), [f"a1.{dc}.lo", f"a1.{dc}.hi"])
            elif _sel == "q1g":
                for dc in range(8):
                    tap(QKVg(dc), [f"kv{dc}"])
            elif _sel == "k1g":
                for dc in range(8):
                    tap(QKVg(8 + dc), [f"kv{8 + dc}"])
        _ck(10)
        b.dma("sync", BBB[0:1, 0:D], fbias_h, writes=["fbr"])
        for half in range(2):
            mm(PS[6][:, 0:512], CN[0:1, 0:128], BBB[0:1, half * 512:(half + 1) * 512],
               True, True, ["fbr"], [PK[6]])
            vop(lambda e, _h=half: e.tensor_copy(
                BBB[:, 1024 + _h * 512:1024 + (_h + 1) * 512], PS[6][:, 0:512]),
                [PK[6], "fbr"], [f"fbb{half}"])
        for kc in range(8):
            wt = U2[:, 0:1024]
            b.dma("sync", wt, fWT[kc * 128:(kc + 1) * 128, :], reads=["fwt"],
                  writes=["fwt"])
            vop(lambda e: e.tensor_reduce(U3[:, 0:1], U2[:, 0:1024], axis=AX.X,
                                          op=ALU.max, apply_absolute_value=True),
                ["fwt"], ["wam1"])
            if kc == 0:
                vop(lambda e: e.tensor_copy(U3[:, 1:2], U3[:, 0:1]), ["wam1"], ["wam"])
            else:
                vop(lambda e: e.tensor_max(U3[:, 1:2], U3[:, 1:2], U3[:, 0:1]),
                    ["wam1", "wam"], ["wam"])
        mm(PS[5][0:1, 0:128], U3[:, 1:2], IDn[:, :], True, True, ["wam", "IDn"], [PK[5]])
        vop(lambda e: e.tensor_reduce(U3[0:1, 2:3], PS[5][0:1, 0:128], axis=AX.X,
                                      op=ALU.max), [PK[5]], ["wamx"])

        def scale_chain(am, sc, dq, k0, k1, k2):
            vop(lambda e, _a=am: e.tensor_scalar_max(_a, _a, 1e-12), [k0], [k0])
            vop(lambda e, _a=am, _s=sc: e.reciprocal(_s, _a), [k0], [k1])
            vop(lambda e, _s=sc: e.tensor_scalar_mul(_s, _s, 448.0), [k1], [k1])
            vop(lambda e, _s=sc: e.tensor_scalar_min(_s, _s, 448.0), [k1], [k1])
            vop(lambda e, _s=sc, _d=dq: e.reciprocal(_d, _s), [k1], [k2])

        scale_chain(U3[0:1, 2:3], U3[0:1, 3:4], U3[0:1, 4:5], "wamx", "wsc", "wdq")

        def bc1(src, srck, dst, dstk, psi=5):
            mm(PS[psi][:, 0:1], CN[0:1, 0:128], src, True, True, [srck], [PK[psi]])
            vop(lambda e, _d=dst: e.tensor_copy(_d, PS[psi][:, 0:1]), [PK[psi]], [dstk])

        vop(lambda e: e.tensor_scalar_mul(U3[0:1, 5:6], U3[0:1, 3:4], 0.5),
            ["wsc"], ["wsch"])
        bc1(U3[0:1, 5:6], "wsch", U3[:, 6:7], "wschb")
        bc1(U3[0:1, 3:4], "wsc", U3[:, 7:8], "wscb")

        def quant(src, srck, dst, dstk, n, shb, shbk, scb, scbk, uscr, uscrk):
            vop(lambda e, _s=src, _h=shb, _n=n: e.tensor_scalar(
                F8s[:, 0:_n], _s, _h, None, ALU.mult), [srck, shbk], ["qc1"])
            sop(lambda e, _n=n: e.activation(XQB[:, 10240:10240 + _n], F8s[:, 0:_n],
                                             AF.Identity, bias=0.0, scale=2.0),
                ["qc1"], ["qc1b"])
            vop(lambda e, _s=src, _c=scb, _u=uscr, _n=n: e.tensor_scalar(
                _u, _s, _c, 16384.0, ALU.mult, ALU.add), [srck, scbk], [uscrk])
            vop(lambda e, _u=uscr, _n=n: e.tensor_scalar_add(
                XQB[:, 11264:11264 + _n], _u, -16384.0), [uscrk], ["qu"])
            sop(lambda e, _u=uscr, _n=n: e.activation(
                _u, XQB[:, 11264:11264 + _n], AF.Square, bias=0.0, scale=1.0),
                ["qu"], [uscrk + "s"])
            vop(lambda e, _u=uscr, _n=n: e.tensor_scalar(
                M8[:, 0:_n], _u, 2.0 ** -12, None, ALU.is_ge), [uscrk + "s"], ["qm"])
            vop(lambda e, _d=dst, _n=n: e.select(
                _d, M8[:, 0:_n], XQB[:, 10240:10240 + _n], XQB[:, 11264:11264 + _n]),
                ["qm", "qc1b", "qu"], [dstk])

        for kc in range(8):
            wt = U2[:, 0:1024]
            b.dma("sync", wt, fWT[kc * 128:(kc + 1) * 128, :], reads=["fwt"],
                  writes=["fwt"])
            quant(wt, "fwt", XQB[:, 2048 + kc * 1024:2048 + (kc + 1) * 1024],
                  f"wq{kc}", 1024, U3[:, 6:7], "wschb", U3[:, 7:8], "wscb",
                  U2[:, 1024:2048], "quw")
        b.dma("sync", U3[0:1, 8:9], am_gd, reads=["amg"], writes=["axg"])
        scale_chain(U3[0:1, 8:9], U3[0:1, 9:10], U3[0:1, 10:11], "axg", "xsc", "xdq")
        vop(lambda e: e.tensor_scalar_mul(U3[0:1, 11:12], U3[0:1, 9:10], 0.5),
            ["xsc"], ["xsch"])
        bc1(U3[0:1, 11:12], "xsch", U3[:, 12:13], "xschb")
        bc1(U3[0:1, 9:10], "xsc", U3[:, 13:14], "xscb")
        vop(lambda e: e.tensor_mul(U3[0:1, 14:15], U3[0:1, 10:11], U3[0:1, 4:5]),
            ["xdq", "wdq"], ["dqf"])
        bc1(U3[0:1, 14:15], "dqf", U3[:, 15:16], "dqfb")
        if UPTO < 10.5:
            tap(U3[0:128, 0:16], ["dqfb", "xschb", "xscb", "wschb", "wscb"])
        _ck(10.5)
        b.barrier()
        for tc in range(16):
            for kc in range(8):
                vrec = U4[:, 128:256]
                vop(lambda e, _v=vrec, _kc=kc, _tc=tc: e.tensor_add(
                    _v, xh(_kc, _tc * 128, (_tc + 1) * 128),
                    xl(_kc, _tc * 128, (_tc + 1) * 128)),
                    ["vrec"], ["vrec"])
                quant(vrec, "vrec",
                      XQB[:, kc * 128:(kc + 1) * 128], f"xq{kc}", 128,
                      U3[:, 12:13], "xschb", U3[:, 13:14], "xscb",
                      U4[:, 0:128], "qux")
            for half in range(2):
                pt, pk = PS[(tc * 2 + half) % 2], PK[(tc * 2 + half) % 2]
                for kc in range(8):
                    mm(pt[:, :], XQB[:, kc * 128:(kc + 1) * 128],
                       XQB[:, 2048 + kc * 1024 + half * 512:
                           2048 + kc * 1024 + (half + 1) * 512],
                       kc == 0, kc == 7, [f"xq{kc}", f"wq{kc}"], [pk])
                ob = XQB[:, 1024 + half * 512:1024 + (half + 1) * 512]
                vop(lambda e, _p=pt, _h=half, _o=ob: e.scalar_tensor_tensor(
                    _o, _p[:, :], U3[:, 15:16], BBB[:, 1024 + _h * 512:1024 + (_h + 1) * 512],
                    ALU.mult, ALU.add), [pk, "dqfb", f"fbb{half}"], ["l1o"])
                gt = U2[:, 1024 + half * 512:1024 + (half + 1) * 512]
                sop(lambda e, _o=ob, _g=gt: e.activation(_g, _o, AF.Silu, bias=0.0,
                                                         scale=1.0), ["l1o"], ["l1g"])
                s1t = U2[:, 2048 + half * 512:2048 + (half + 1) * 512]
                b.dma("sync", s1t,
                      s1_d[tc * 128:(tc + 1) * 128, half * 512:(half + 1) * 512],
                      reads=["l1s1"], writes=["l1s1"])
                x1t = U2[:, 3072 + half * 512:3072 + (half + 1) * 512]
                b.dma("sync", x1t,
                      x1_d[tc * 128:(tc + 1) * 128, half * 512:(half + 1) * 512],
                      reads=["l1x1"], writes=["l1x1"])
                vop(lambda e, _s=s1t, _g=gt: e.tensor_mul(_s, _s, _g),
                    ["l1s1", "l1g"], ["l1sg"])
                ot = U2[:, half * 512:(half + 1) * 512]
                _dbg = os.environ.get("K_OUTSEL", "")
                if _dbg == "xq":
                    sop(lambda e, _o=ot, _h=half: e.activation(
                        _o, XQB[:, _h * 512:(_h + 1) * 512], AF.Identity,
                        bias=0.0, scale=1.0),
                        [f"xq{k}" for k in range(8)], ["ott"])
                elif _dbg == "g1":
                    vop(lambda e, _o=ot, _g=gt: e.tensor_copy(_o, _g),
                        ["l1g"], ["ott"])
                elif _dbg == "s1":
                    vop(lambda e, _o=ot, _s=s1t: e.tensor_copy(_o, _s),
                        ["l1sg"], ["ott"])
                else:
                    vop(lambda e, _o=ot, _s=s1t, _x=x1t: e.tensor_add(_o, _x, _s),
                        ["l1sg", "l1x1"], ["ott"])
                b.dma("gpsimd",
                      out_h[tc * 128:(tc + 1) * 128, half * 512:(half + 1) * 512],
                      ot, reads=["ott"], writes=["outd"])
        b.barrier()

    except _Done:
        b.barrier()

    block = e_(nc.Block())
    b.junk = (junk_v, junk_s, junk_g)
    b.finalize(nc, block)
    ctx.close()
    return nc


def kernel(**inputs):
    f32 = np.float32
    x = np.ascontiguousarray(np.asarray(inputs["L0_attn_x"]).reshape(16384, 1024)).astype(f32, copy=False)
    tr = lambda a: np.ascontiguousarray(np.asarray(a).astype(f32, copy=False).T)
    common = {
        "wqT": tr(inputs["L0_attn_Wq"]), "wkT": tr(inputs["L0_attn_Wk"]),
        "wvT": tr(inputs["L0_attn_Wv"]), "woutT": tr(inputs["L0_attn_Wout"]),
        "bout": np.asarray(inputs["L0_attn_bout"], f32).reshape(1, D),
        "gate_wT": tr(inputs["L0_ffn_gate_w"]),
        "w1T": np.ascontiguousarray(np.asarray(inputs["L0_ffn_w1"], f32).transpose(0, 2, 1)),
        "w3T": np.ascontiguousarray(np.asarray(inputs["L0_ffn_w3"], f32).transpose(0, 2, 1)),
        "w2T": np.ascontiguousarray(np.asarray(inputs["L0_ffn_w2"], f32).transpose(0, 2, 1)),
        "sw1T": tr(inputs["L0_ffn_sw1"]), "sw3T": tr(inputs["L0_ffn_sw3"]),
        "sw2T": tr(inputs["L0_ffn_sw2"]),
        "wqkv": np.ascontiguousarray(np.asarray(inputs["L1_attn_w_qkv"], f32)),
        "ln_w": np.asarray(inputs["L1_attn_ln_w"], f32).reshape(1, D),
        "wproj": np.ascontiguousarray(np.asarray(inputs["L1_attn_w_proj"], f32)),
        "fWT": tr(inputs["L1_ffn_weight"]),
        "fbias": np.asarray(inputs["L1_ffn_bias"], f32).reshape(1, D),
        "ident": np.eye(128, dtype=f32),
        "ones_r": np.ones((1, 128), f32),
        "bd": np.repeat(np.eye(NH, dtype=f32), HDim, axis=0),
        "maskT": np.ascontiguousarray(np.triu(np.ones((GS, GS), f32))),
    }
    nc = build_nc()
    in_maps = []
    for c in range(8):
        m = dict(common)
        m["x"] = np.ascontiguousarray(x[c * T:(c + 1) * T])
        in_maps.append(m)
    res = run_bass_kernel_spmd(nc, in_maps, core_ids=list(range(8)))
    out = np.concatenate([res.results[c]["out"] for c in range(8)], axis=0)
    return out.reshape(2, 8192, 1024)
